# version 03
# baseline (speedup 1.0000x reference)
import numpy as np
from contextlib import ExitStack
import concourse.bass as bass
import concourse.mybir as mybir
from concourse.bacc import Bacc

F32 = mybir.dt.float32
BF16 = mybir.dt.bfloat16
I16 = mybir.dt.int16
AF = mybir.ActivationFunctionType
ALU = mybir.AluOpType
AX = mybir.AxisListType
MAGIC = 12582912.0

ENGS = ("sync", "scalar", "vector", "tensor", "gpsimd")
SEQ, T, C, H, D = 2, 2048, 1024, 16, 64
RC = T // 128
NCC = C // 128
HID = 4096
EPS_LN = 1e-5
EPS_RMS = 1e-8
EPS_Q = 1e-5


class Sem:
    __slots__ = ("h", "n")
    def __init__(self, h):
        self.h = h
        self.n = 0


def _ap_names(obj, acc):
    t = getattr(obj, "tensor", None)
    if t is not None and hasattr(t, "name"):
        acc.add(t.name)
        return
    if isinstance(obj, (list, tuple)):
        for o in obj:
            _ap_names(o, acc)



def _emit_waits(e, waits):
    best = {}
    for ws, wv in waits:
        key = id(ws)
        if key not in best or best[key][1] < wv:
            best[key] = (ws, wv)
    for ws, wv in best.values():
        e.wait_ge(ws.h, wv)


class _AutoFence:
    _PASS = ("wait_ge", "wait_eq", "wait_le")

    def __init__(self, e, k, eng):
        self._e = e
        self._k = k
        self._esem = k.esem[eng]
        self._fence_sem, self._fence_cnt = k.fsems[eng]
        self._hist = k.fhist[eng]
        self._pend = None

    def _flush_pend(self):
        inst, names = self._pend
        self._fence_cnt[0] += 1
        inst.then_inc(self._fence_sem.h, 1)
        self._hist.append((names, "f", self._fence_cnt[0]))
        if len(self._hist) > 64:
            del self._hist[0:len(self._hist) - 64]
        self._pend = None

    def __getattr__(self, nm):
        a = getattr(self._e, nm)
        if not callable(a) or nm.startswith("_") or nm in self._PASS:
            return a
        def call(*args, **kw):
            names = set()
            for o in args:
                _ap_names(o, names)
            for o in kw.values():
                _ap_names(o, names)
            if self._pend is not None:
                self._flush_pend()
            need_f = need_e = 0
            for nm_, kind, val in self._hist:
                if names & nm_:
                    if kind == "f":
                        need_f = max(need_f, val)
                    else:
                        need_e = max(need_e, val)
            if need_f:
                self._e.wait_ge(self._fence_sem.h, need_f)
            if need_e:
                self._e.wait_ge(self._esem.h, need_e)
            if need_f or need_e:
                self._hist[:] = [r for r in self._hist
                                 if (r[2] > need_f if r[1] == "f" else r[2] > need_e)]
            inst = a(*args, **kw)
            if inst is not None and hasattr(inst, "then_inc"):
                self._pend = (inst, names)
            return inst
        return call

    def _finish(self, last, tok_val):
        if self._pend is not None:
            inst, names = self._pend
            self._pend = None
            if inst is last:
                self._hist.append((names, "e", tok_val))
                if len(self._hist) > 64:
                    del self._hist[0:len(self._hist) - 64]
                return
            self._pend = (inst, names)
            self._flush_pend()


class K:
    def __init__(self, nc, es):
        self.nc = nc
        self.es = es
        self.ops = {e: [] for e in ENGS}
        self._semi = 0
        self.esem = {e: self.sem(f"q_{e}") for e in ENGS}
        self.dpool = {e: [[self.sem(f"d_{e}{i}"), 0, None] for i in range(8)]
                      for e in ("sync", "gpsimd", "scalar")}
        self.didx = {e: 0 for e in self.dpool}
        self.fsem = self.sem("fence")
        self.fn_ = [0]
        self.fsems = {e: (self.sem(f"f_{e}"), [0])
                      for e in ("vector", "scalar", "gpsimd", "tensor", "sync")}
        self.fhist = {e: [] for e in self.fsems}

    def sem(self, name=None):
        self._semi += 1
        return Sem(self.es.enter_context(self.nc.semaphore(name or f"s{self._semi}")))

    def sbuf(self, name, shape, dt, es=None):
        return (es or self.es).enter_context(self.nc.sbuf_tensor(name, shape, dt))

    def psum(self, name, shape, dt=F32, es=None):
        return (es or self.es).enter_context(self.nc.psum_tensor(name, shape, dt))

    def op(self, eng, fn, waits=()):
        s = self.esem[eng]
        s.n += 1
        tgt = s.n
        waits = tuple(w for w in waits if w is not None)
        fence_eng = eng in ("vector", "scalar", "gpsimd")
        def run(e, fn=fn, waits=waits, s=s, fence_eng=fence_eng, tgt=tgt, eng=eng):
            _emit_waits(e, waits)
            if fence_eng:
                px = _AutoFence(e, self, eng)
                last = fn(px)
                assert last is not None, f"op on {eng} returned no instruction"
                assert px._pend is not None and px._pend[0] is last, \
                    f"op on {eng} must return its final instruction"
                last.then_inc(s.h, 1)
                px._finish(last, tgt)
            else:
                last = fn(e)
                assert last is not None, f"op on {eng} returned no instruction"
                last.then_inc(s.h, 1)
        self.ops[eng].append(run)
        return (s, tgt)

    def _dslot(self, eng, n_incs):
        slot = self.dpool[eng][self.didx[eng]]
        self.didx[eng] = (self.didx[eng] + 1) % len(self.dpool[eng])
        prev = (slot[0], slot[1]) if slot[1] else None
        slot[1] += 16 * n_incs
        return slot[0], slot[1], prev

    def dma(self, eng, dst, src, waits=()):
        s, tgt, prev = self._dslot(eng, 1)
        waits = tuple(w for w in waits if w is not None) + ((prev,) if prev else ())
        def run(e, dst=dst, src=src, waits=waits, s=s):
            _emit_waits(e, waits)
            e.dma_start(out=dst, in_=src).then_inc(s.h, 16)
        self.ops[eng].append(run)
        return (s, tgt)

    def dma_multi(self, eng, pairs, waits=()):
        s, tgt, prev = self._dslot(eng, len(pairs))
        waits = tuple(w for w in waits if w is not None) + ((prev,) if prev else ())
        def run(e, pairs=pairs, waits=waits, s=s):
            _emit_waits(e, waits)
            for dst, src in pairs:
                e.dma_start(out=dst, in_=src).then_inc(s.h, 16)
        self.ops[eng].append(run)
        return (s, tgt)

    def dma_drain(self, eng):
        return [(sl[0], sl[1]) for sl in self.dpool[eng] if sl[1]]

    def wait_only(self, eng, waits):
        waits = tuple(w for w in waits if w is not None)
        def run(e, waits=waits):
            _emit_waits(e, waits)
        self.ops[eng].append(run)

    def realize(self, name=None):
        with self.nc.Block(name) as block:
            for eng in ENGS:
                ops = self.ops[eng]
                if not ops:
                    continue
                getattr(block, eng)(lambda e, ops=ops: [f(e) for f in ops])
        self.ops = {e: [] for e in ENGS}




def FENCE(k, e, inst):
    if getattr(e, "_fence_sem", None) is not None:
        return
    k.fn_[0] += 1
    inst.then_inc(k.fsem.h, 1)
    e.wait_ge(k.fsem.h, k.fn_[0])


def _copy(e, out, in_):
    if hasattr(e, "tensor_copy"):
        return e.tensor_copy(out=out, in_=in_)
    return e.copy(out=out, in_=in_)


def seq_ln(k, x_tile, outs, prod_scr, scr, waits, eps=EPS_LN, w_extra=()):
    W = x_tile.shape[1]
    def vf(v):
        v.reduce_sum(scr[:, 0:1], x_tile[:, :], axis=AX.X)
        v.affine_mul_reduce(prod_scr[:, 0:W], scr[:, 4:5], x_tile[:, :], x_tile[:, :], 1.0, 0.0)
        v.tensor_scalar(scr[:, 0:1], scr[:, 0:1], 1.0 / W, None, ALU.mult)
        v.tensor_scalar(scr[:, 4:5], scr[:, 4:5], 1.0 / W, None, ALU.mult)
        i0 = v.tensor_tensor(scr[:, 1:2], scr[:, 0:1], scr[:, 0:1], ALU.mult)
        FENCE(k, v, i0)
        i1 = v.tensor_tensor(scr[:, 5:6], scr[:, 4:5], scr[:, 1:2], ALU.subtract)
        FENCE(k, v, i1)
        i2 = v.tensor_scalar(scr[:, 2:3], scr[:, 5:6], float(eps), None, ALU.add)
        v.tensor_copy(out=scr[:, 1:2], in_=scr[:, 5:6])
        return v.reciprocal(scr[:, 3:4], scr[:, 2:3])
    t1 = k.op("vector", vf, waits=waits)
    t2 = k.op("scalar", lambda s: s.activation(scr[:, 2:3], scr[:, 3:4], AF.Sqrt), waits=[t1])
    t3 = k.op("vector", lambda v: v.tensor_scalar(scr[:, 0:1], scr[:, 0:1], -1.0, None, ALU.mult),
              waits=[t1])
    def vf3(v):
        last = None
        for ot in outs:
            last = v.tensor_scalar(ot, x_tile[:, :], scr[:, 0:1], scr[:, 2:3], ALU.add, ALU.mult)
        return last
    t4 = k.op("vector", vf3, waits=[t2, t3] + list(w_extra))
    return t4, t1


def seq_transpose(k, src_tile, cols, dst_fn, ident, tpsum, tpsum2, waits, evac_eng="vector"):
    groups = [cols[i:i+4] for i in range(0, len(cols), 4)]
    banks = [tpsum, tpsum2]
    def view(b):
        return b[:, :].bitcast(BF16) if src_tile.dtype == BF16 else b[:, :]
    tts = []
    evs = []
    for gi, grp in enumerate(groups):
        ps = view(banks[gi % 2])
        def tf(t, grp=grp, ps=ps):
            last = None
            for jj, j in enumerate(grp):
                last = t.transpose(ps[:, jj*128:jj*128+128], src_tile[:, j*128:(j+1)*128], ident)
            return last
        w = list(waits) + ([evs[gi-2]] if gi >= 2 else [])
        tts.append(k.op("tensor", tf, waits=w))
        if gi >= 1:
            psp = view(banks[(gi-1) % 2])
            def ef(e, grp=groups[gi-1], psp=psp):
                last = None
                for jj, j in enumerate(grp):
                    last = _copy(e, dst_fn(j), psp[:, jj*128:jj*128+128])
                return last
            evs.append(k.op(evac_eng, ef, waits=[tts[gi]]))
    other = view(banks[len(groups) % 2])
    def settle(t):
        return t.transpose(other[:, 0:128], src_tile[:, cols[0]*128:cols[0]*128+128], ident)
    w_st = [tts[-1]] + ([evs[len(groups)-2]] if len(groups) >= 2 else [])
    t_settle = k.op("tensor", settle, waits=w_st)
    psl = view(banks[(len(groups)-1) % 2])
    def ef_last(e, grp=groups[-1], psl=psl):
        last = None
        for jj, j in enumerate(grp):
            last = _copy(e, dst_fn(j), psl[:, jj*128:jj*128+128])
        return last
    tok = k.op(evac_eng, ef_last, waits=[t_settle])
    return tok


def seq_mm(k, psum_out, lhsT_slices, rhs_slices, waits):
    def tf(t):
        n = len(lhsT_slices)
        last = None
        for i, (a, b) in enumerate(zip(lhsT_slices, rhs_slices)):
            last = t.matmul(psum_out, a, b, start=(i == 0), stop=(i == n - 1))
        return last
    return k.op("tensor", tf, waits=waits)


def seq_round_to(k, src, dst, scale_ap, waits):
    t1 = k.op("scalar", lambda s: s.activation(src, src, AF.Copy, bias=MAGIC, scale=scale_ap), waits=waits)
    t2 = k.op("vector", lambda v: v.tensor_scalar(dst, src, -MAGIC, None, ALU.add), waits=[t1])
    return t2



DEBUG_DUMP = False
A1_ONLY = False
FFN_DUMP = False
INTERNAL_STAGE = False

def build_phaseA(nc, es, IN, x3_out):
    k = K(nc, es)
    xin = IN["xin"]
    dbg = nc.dram_tensor("dbg", [128, 4096], F32, kind="ExternalOutput").ap() if DEBUG_DUMP else None
    a1dump = nc.dram_tensor("a1dump", [RC, 128, 4096], F32, kind="ExternalOutput").ap() if A1_ONLY else None
    dbg2 = nc.dram_tensor("dbg2", [128, 2048], mybir.dt.bfloat16, kind="ExternalOutput").ap() if DEBUG_DUMP else None
    dbg3 = nc.dram_tensor("dbg3", [16, 2400], F32, kind="ExternalOutput").ap() if DEBUG_DUMP else None

    _kind = "Internal" if INTERNAL_STAGE else "ExternalOutput"
    qts = nc.dram_tensor("qts", [H, D + 1, T], BF16, kind=_kind).ap()
    x2s = nc.dram_tensor("x2s", [SEQ, T, C], F32, kind=_kind).ap()

    ident_b = k.sbuf("ident_b", [128, 128], BF16)
    ident_f = k.sbuf("ident_f", [128, 128], F32)
    ones_b = k.sbuf("ones_b", [65, 128], BF16)
    def cinit(g):
        g.memset(ident_b[:, :], 1.0)
        g.memset(ident_f[:, :], 1.0)
        g.memset(ones_b[:, :], 1.0)
        g.affine_select(ident_b[:, :], ident_b[:, :], pattern=[[1, 128]], compare_op=ALU.is_equal,
                        fill=0.0, base=0, channel_multiplier=-1)
        return g.affine_select(ident_f[:, :], ident_f[:, :], pattern=[[1, 128]], compare_op=ALU.is_equal,
                               fill=0.0, base=0, channel_multiplier=-1)
    t_const = k.op("gpsimd", cinit)
    if DEBUG_DUMP:
        k.dma("gpsimd", dbg2[:, 1604:1732], ident_b[:, :], waits=[t_const])

    esA = ExitStack()
    wqkv_b = k.sbuf("wqkv_b", [128, NCC * 3072], BF16, es=esA)
    wproj_b = k.sbuf("wproj_b", [128, NCC * 1024], BF16, es=esA)
    bq_a = k.sbuf("bq_a", [65, 512], BF16, es=esA)
    bq_b = k.sbuf("bq_b", [65, 512], BF16, es=esA)
    bp_a = k.sbuf("bp_a", [33, 512], BF16, es=esA)
    def brow(fb):
        t_ = bq_a if fb < 3 else bq_b
        return t_[(fb % 3)*32:(fb % 3)*32 + 1, 0:512]
    def bprow(fb):
        return bp_a[fb*32:fb*32 + 1, 0:512]
    t_wq = k.dma_multi("gpsimd", [(wqkv_b[:, cc*3072:(cc+1)*3072], IN["w_qkv"][cc*128:(cc+1)*128, :])
                                  for cc in range(NCC)])
    t_wp = k.dma_multi("gpsimd", [(wproj_b[:, cc*1024:(cc+1)*1024], IN["w_proj"][cc*128:(cc+1)*128, :])
                                  for cc in range(NCC)])
    t_b1 = k.dma_multi("gpsimd",
                       [(brow(fb), IN["b_qkv"][0:1, fb*512:(fb+1)*512]) for fb in range(6)]
                       + [(bprow(fb), IN["b_proj"][0:1, fb*512:(fb+1)*512]) for fb in range(2)])

    KT = k.sbuf("KT", [65, H * T], BF16, es=esA)
    Vp = k.sbuf("Vp", [128, RC * H * 65], BF16, es=esA)
    qnT = k.sbuf("qnT", [16, T], BF16, es=esA)
    maxk = k.sbuf("maxk", [16, 4], F32, es=esA)
    OT = k.sbuf("OT", [128, NCC * 512], BF16, es=esA)
    sums_t = k.sbuf("sums_t", [16, 512], F32, es=esA)
    rsum_t = k.sbuf("rsum_t", [16, 512], F32, es=esA)
    rb64 = k.sbuf("rb64", [128, 512], F32, es=esA)

    scr = k.sbuf("scr", [128, 16], F32, es=esA)
    ln_f = k.sbuf("ln_f", [128, 1024], F32, es=esA)
    lnT = k.sbuf("lnT", [128, 1024], BF16, es=esA)
    qkv_tm = k.sbuf("qkv_tm", [128, 3072], F32, es=esA)
    nrm = k.sbuf("nrm", [128, 32], F32, es=esA)
    xrrc = k.sbuf("xrrc", [128, 1024], F32, es=esA)
    Qa = [k.sbuf(f"Qa{i}", [65, 512], BF16, es=esA) for i in range(2)]
    Pt0 = k.sbuf("Pt0", [128, 512], BF16, es=esA)
    Pt = [Pt0, Pt0]

    tp1 = k.psum("tp1", [128, 512], F32, es=esA)
    tp2 = k.psum("tp2", [128, 512], F32, es=esA)
    mmp = [k.psum(f"mmp{i}", [128, 512], F32, es=esA) for i in range(2)]
    sp = [k.psum(f"sp{i}", [128, 512], F32, es=esA) for i in range(2)]
    op_ps = k.psum("op_ps", [65, 512], F32, es=esA)
    pp_ps = k.psum("pp_ps", [128, 512], F32, es=esA)

    t_seq_end = None
    for s in range(SEQ):
        iter_tok = [t_seq_end] if t_seq_end else []
        t_kq_last = None
        t_vv_last = None
        t_vv_prev = None
        t_qkv_bank_prev = {}
        t_ln_hist = {}
        t_lnT_hist = {}
        t_x_hist = {}
        for rc in range(RC):
            t_x = k.dma("sync", xrrc[:, :], xin[s, rc*128:(rc+1)*128, :],
                        waits=iter_tok + ([t_ln_hist.get(rc-1)] if rc >= 1 else []))
            iter_tok = []
            t_x_hist[rc] = t_x
            t_ln, _ = seq_ln(k, xrrc, [ln_f[:, :]], qkv_tm[:, 2048:3072], scr, [t_x],
                             w_extra=[t_lnT_hist.get(rc-1)] if rc >= 1 else [])
            t_ln_hist[rc] = t_ln
            if DEBUG_DUMP and s == 0:
                k.dma("gpsimd", dbg3[:, 2060+rc*16:2060+rc*16+8], scr[0:16, 0:8], waits=[t_ln])
                k.dma("gpsimd", dbg3[:, 2068+rc*16:2068+rc*16+8], ln_f[0:16, 0:8], waits=[t_ln])
            t_lnT = seq_transpose(k, ln_f, list(range(8)), lambda j: lnT[:, j*128:j*128+128],
                                  ident_f[:, :], tp1, tp2, [t_ln, t_const, t_kq_last], evac_eng="scalar")
            t_lnT_hist[rc] = t_lnT
            if DEBUG_DUMP and s == 0 and rc == 15:
                k.dma("gpsimd", dbg2[:, 580:1604], lnT[:, :], waits=[t_lnT])
            t_ev = []
            for fb in range(6):
                pb = mmp[fb % 2]
                w_mm = [t_lnT, t_wq, t_b1] + ([t_ev[-2]] if len(t_ev) >= 2 else [t_qkv_bank_prev.get(fb)])
                t_mm = seq_mm(k, pb[:, :],
                              [lnT[:, cc*128:(cc+1)*128] for cc in range(NCC)]
                              + [ones_b[(fb % 3)*32:(fb % 3)*32 + 1, :]],
                              [wqkv_b[:, cc*3072 + fb*512: cc*3072 + fb*512 + 512] for cc in range(NCC)]
                              + [brow(fb)],
                              w_mm)
                sc = 0.125 if fb < 2 else 1.0
                def ef(e, pb=pb, fb=fb, sc=sc):
                    if sc == 1.0:
                        return e.tensor_copy(out=qkv_tm[:, fb*512:(fb+1)*512], in_=pb[:, :])
                    return e.activation(qkv_tm[:, fb*512:(fb+1)*512], pb[:, :], AF.Copy, scale=sc)
                eng = "scalar" if fb < 2 else "vector"
                t_ev.append(k.op(eng, ef, waits=[t_mm, t_vv_prev]))
            t_qkv = t_ev[-1]
            t_qkv_bank_prev = {0: t_ev[-2], 1: t_ev[-1]}

            def nf(v):
                last = None
                for h in range(H):
                    last = v.affine_mul_reduce(rb64[:, 0:64], nrm[:, h:h+1], qkv_tm[:, h*64:(h+1)*64],
                                               qkv_tm[:, h*64:(h+1)*64], 1.0, 0.0)
                for h in range(H):
                    last = v.affine_mul_reduce(rb64[:, 0:64], nrm[:, 16+h:17+h],
                                               qkv_tm[:, 1024+h*64:1024+(h+1)*64],
                                               qkv_tm[:, 1024+h*64:1024+(h+1)*64], 1.0, 0.0)
                return last
            t_n1 = k.op("vector", nf, waits=t_ev)
            if DEBUG_DUMP and s == 0 and rc == 15:
                k.dma("gpsimd", dbg[:, 0:1024], ln_f[:, :], waits=[t_ln])
                k.dma("gpsimd", dbg2[:, 516:548], nrm[:, 0:32], waits=[t_n1])
            t_n2 = k.op("scalar", lambda s_: s_.activation(nrm[:, 0:32], nrm[:, 0:32], AF.Sqrt), waits=[t_n1])
            if DEBUG_DUMP and s == 0 and rc == 15:
                k.dma("gpsimd", dbg2[:, 548:580], nrm[:, 0:32], waits=[t_n2])
            def n3(t):
                t.transpose(tp2[0:16, 0:128], nrm[:, 0:16], ident_f[:, :])
                return t.transpose(tp2[0:16, 128:256], nrm[:, 16:32], ident_f[:, :])
            t_n3 = k.op("tensor", n3, waits=[t_n2, t_kq_last])
            def n4(v, rc=rc):
                v.reduce_max(maxk[:, 1:2], tp2[0:16, 128:256], axis=AX.X)
                v.tensor_copy(out=qnT[:, rc*128:(rc+1)*128], in_=tp2[0:16, 0:128])
                if rc == 0:
                    return v.tensor_copy(out=maxk[:, 0:1], in_=maxk[:, 1:2])
                return v.tensor_tensor(maxk[:, 0:1], maxk[:, 0:1], maxk[:, 1:2], ALU.max)
            t_n4 = k.op("vector", n4, waits=[t_n3])
            t_kq = t_n4
            for hg in range(4):
                def tkf(t, hg=hg):
                    last = None
                    for j in range(4):
                        h = hg*4 + j
                        last = t.transpose(tp2[0:64, j*128:(j+1)*128],
                                           qkv_tm[:, 1024 + h*64: 1024 + (h+1)*64], ident_f[:, :])
                    return last
                t_tk = k.op("tensor", tkf, waits=[t_kq])
                def ekf(e, hg=hg, rc=rc):
                    last = None
                    for j in range(4):
                        h = hg*4 + j
                        last = _copy(e, KT[0:64, h*T + rc*128: h*T + rc*128 + 128],
                                     tp2[0:64, j*128:(j+1)*128])
                    return last
                t_ek = k.op("scalar", ekf, waits=[t_tk])
                def tqf(t, hg=hg):
                    last = None
                    for j in range(4):
                        h = hg*4 + j
                        last = t.transpose(tp2[0:64, j*128:(j+1)*128],
                                           qkv_tm[:, h*64:(h+1)*64], ident_f[:, :])
                    return last
                t_tq = k.op("tensor", tqf, waits=[t_ek])
                def eqf(e):
                    last = None
                    for j in range(4):
                        last = _copy(e, Pt[0][0:64, j*128:(j+1)*128], tp2[0:64, j*128:(j+1)*128])
                    return last
                t_eq = k.op("scalar", eqf, waits=[t_tq])
                t_dq = k.dma_multi("gpsimd",
                                   [(qts[hg*4+j, 0:64, rc*128:(rc+1)*128], Pt[0][0:64, j*128:(j+1)*128])
                                    for j in range(4)], waits=[t_eq])
                t_kq = t_dq
            t_kq_last = t_kq
            def vvf(e, rc=rc):
                base = rc * H * 65
                dst = Vp[:, base:base + H*65].rearrange("p (h x) -> p h x", h=H)[:, :, 0:64]
                src = qkv_tm[:, 2048:3072].rearrange("p (h x) -> p h x", h=H)
                e.tensor_copy(out=dst, in_=src)
                return e.memset(Vp[:, base:base + H*65].rearrange("p (h x) -> p h x", h=H)[:, :, 64:65], 1.0)
            t_vv_last = k.op("vector", vvf, waits=[t_qkv])
            t_vv_prev = t_vv_last
            if A1_ONLY and s == 0:
                k.dma("gpsimd", a1dump[rc, 0:128, 0:3072], qkv_tm[:, :], waits=[t_qkv, t_n4, t_kq_last, t_vv_last])
                k.dma("gpsimd", a1dump[rc, 0:128, 3072:3584], lnT[:, :].bitcast(F32), waits=[t_lnT, t_kq_last])
        t_kone = k.op("gpsimd", lambda g: g.memset(KT[64:65, :], 1.0), waits=[t_kq_last])
        if A1_ONLY:
            k.wait_only("gpsimd", k.dma_drain("gpsimd"))
            k.realize("a1only")
            esA.close()
            return k
        if DEBUG_DUMP and s == 0:
            k.dma("gpsimd", dbg3[:, 2052:2056], maxk[0:16, :], waits=[t_n4])
            k.dma("gpsimd", dbg3[:, 2056:2058], qnT[0:16, 2046:2048], waits=[t_n4])
            k.dma("gpsimd", dbg[:, 0+0:32+0] if False else dbg[:, 3072:3104], nrm[:, 0:32], waits=[t_n4])
        t_ball = k.op("vector", lambda v: v.tensor_scalar(qnT[0:16, :], qnT[0:16, :], maxk[0:16, 0:1],
                                                          -1.0, ALU.mult, ALU.mult),
                      waits=[t_n4, t_vv_last])
        t_bdma = k.dma_multi("sync", [(qts[h, 64, :], qnT[h:h+1, :]) for h in range(H)], waits=[t_ball])

        if DEBUG_DUMP and s == 0:
            k.dma("gpsimd", dbg2[0:65, 0:256], KT[:, 0:256], waits=[t_kq_last, t_kone])
            k.dma("gpsimd", dbg2[:, 256:256+65*4], Vp[:, 0:260], waits=[t_vv_last])
            k.dma("gpsimd", dbg3[:, 0:2048], qnT[0:16, :], waits=[t_bdma])
            k.dma("gpsimd", dbg3[:, 2048:2052], maxk[0:16, :], waits=[t_bdma])
        a1_done = [t_kq_last, t_vv_last, t_kone, t_bdma]
        t_pe_prev = None
        t_prev_norm = None
        t_x2_done = []
        for qb in range(4):
            nkc = 4 * (qb + 1)
            t_oun = []
            t_exp_prev = None
            t_pv_prev = None
            t_og = None
            for h in range(H):
                qa = Qa[h % 2]
                guard = [t_oun[h-2]] if h >= 2 else ([t_prev_norm] if t_prev_norm else [])
                t_dq = k.dma("gpsimd", qa[0:65, :], qts[h, :, qb*512:(qb+1)*512], waits=guard + a1_done)
                t_bq = t_dq
                t_exp = {}
                t_pv_by = {}
                t_pv = None
                for kc in range(nkc):
                    spb = sp[kc % 2]
                    pt = Pt[kc % 2]
                    w_s = [t_dq, t_bq]
                    if kc >= 2:
                        w_s.append(t_exp[kc-2])
                    elif t_exp_prev is not None:
                        w_s.append(t_exp_prev)
                    def smf(t, h=h, kc=kc, qa=qa, spb=spb):
                        return t.matmul(spb[:, :], KT[:, h*T + kc*128: h*T + kc*128 + 128], qa[0:65, :],
                                        start=True, stop=True)
                    t_sm = k.op("tensor", smf, waits=w_s)
                    def exf(s_, spb=spb, pt=pt):
                        return s_.activation(pt[:, :], spb[:, :], AF.Exp)
                    w_e = [t_sm] + ([t_pv_by[kc-1]] if kc >= 1 else ([t_pv_prev] if t_pv_prev else []))
                    t_ex = k.op("scalar", exf, waits=w_e)
                    t_exp[kc] = t_ex
                    if kc >= 4 * qb:
                        def mkf(g, pt=pt, qb=qb, kc=kc):
                            return g.affine_select(pt[:, :], pt[:, :], pattern=[[1, 512]],
                                                   compare_op=ALU.is_ge, fill=0.0,
                                                   base=qb*512 - kc*128, channel_multiplier=-1)
                        t_mk = k.op("gpsimd", mkf, waits=[t_ex])
                        w_pv = [t_mk]
                    else:
                        w_pv = [t_ex]
                    if kc == 0 and t_oun:
                        w_pv = w_pv + [t_oun[-1]]
                    def pvf(t, h=h, kc=kc, pt=pt, nkc=nkc):
                        return t.matmul(op_ps[0:65, :], Vp[:, (kc*H + h)*65:(kc*H + h)*65 + 65], pt[:, :],
                                        start=(kc == 0), stop=(kc == nkc - 1))
                    t_pv = k.op("tensor", pvf, waits=w_pv)
                    t_pv_by[kc] = t_pv
                def ouf(v, h=h):
                    v.tensor_copy(out=OT[(h % 2)*64:(h % 2)*64 + 64, (h//2)*512:(h//2)*512 + 512],
                                  in_=op_ps[0:64, :])
                    return v.tensor_copy(out=rsum_t[0:1, :], in_=op_ps[64:65, :])
                t_o0 = k.op("vector", ouf, waits=[t_pv] + ([t_og] if h >= 1 else []))
                t_og = k.dma("gpsimd", sums_t[h:h+1, :], rsum_t[0:1, :], waits=[t_o0])
                t_o = t_og
                t_oun.append(t_o)
                t_exp_prev = t_exp[nkc-1]
                t_pv_prev = t_pv
            t_rc_ = k.op("vector", lambda v: v.reciprocal(rsum_t[:, :], sums_t[:, :]), waits=[t_oun[-1]])
            t_nm = []
            for h in range(H):
                t_rbd = k.dma("gpsimd", rsum_t[0:1, :], rsum_t[h:h+1, :],
                              waits=[t_rc_] + ([t_nm[-1]] if t_nm else []))
                t_pb = k.op("gpsimd", lambda g: g.partition_broadcast(rb64[:, :], rsum_t[0:1, :],
                                                                      channels=128),
                            waits=[t_rbd])
                def nmf(e, h=h):
                    b = (h % 2)*64
                    sl = OT[b:b+64, (h//2)*512:(h//2)*512+512]
                    return e.tensor_tensor(sl, sl, rb64[b:b+64, :], ALU.mult)
                t_nm.append(k.op("vector", nmf, waits=[t_pb]))
            t_prev_norm = t_nm[-1]
            for j in range(4):
                rows = qb*512 + j*128
                w_rx = t_x2_done[-1:] if t_x2_done else [t_ln_hist[RC-1], t_kq_last, t_vv_last]
                t_rx = k.dma("sync", xrrc[:, :], xin[s, rows:rows+128, :], waits=w_rx)
                t_pe = []
                for fb in range(2):
                    t_mm = seq_mm(k, pp_ps[:, :],
                                  [OT[:, cc*512 + j*128: cc*512 + j*128 + 128] for cc in range(NCC)]
                                  + [ones_b[fb*32:fb*32 + 1, :]],
                                  [wproj_b[:, cc*1024 + fb*512: cc*1024 + fb*512 + 512] for cc in range(NCC)]
                                  + [bprow(fb)],
                                  [t_nm[-1], t_wp, t_b1, t_pe_prev] + t_pe)
                    def epf(e, fb=fb):
                        sl = xrrc[:, fb*512:(fb+1)*512]
                        return e.tensor_tensor(sl, sl, pp_ps[:, :], ALU.add)
                    t_pe.append(k.op("vector", epf, waits=[t_mm, t_rx]))
                    t_pe_prev = t_pe[-1]
                t_wr = k.dma("gpsimd", x2s[s, rows:rows+128, :], xrrc[:, :], waits=t_pe)
                t_x2_done.append(t_wr)
        t_seq_end = t_x2_done[-1]
        k.realize(f"attn_seq{s}")
    esA.close()

    build_l0f(k, x2s, x3_out, IN, ident_b, ident_f, ones_b)
    return k


def build_l0f(k, x2s, x3_out, IN, ident_b, ident_f, ones_b):
    nc = k.nc
    esF = ExitStack()
    if FFN_DUMP:
        fscr_d = nc.dram_tensor("fd_scr", [SEQ * RC, 128, 16], F32, kind="ExternalOutput").ap()
        fxq_d = nc.dram_tensor("fd_xqb", [SEQ * RC, 128, C], BF16, kind="ExternalOutput").ap()
        fhq_d = nc.dram_tensor("fd_hqb", [SEQ * RC, 128, HID], BF16, kind="ExternalOutput").ap()
    w1q = k.sbuf("w1q", [128, NCC * HID], BF16, es=esF)
    w2q = k.sbuf("w2q", [128, (HID//128) * C], BF16, es=esF)
    wst = k.sbuf("wst", [128, HID], F32, es=esF)
    wabs = k.sbuf("wabs", [128, 2], F32, es=esF)
    wsc = k.sbuf("wsc", [128, 4], F32, es=esF)
    wsc_s = k.sbuf("wsc_s", [1, 4], F32, es=esF)
    wred = k.sbuf("wred", [1, 128], F32, es=esF)
    wp_ps = k.psum("wp_ps", [1, 128], F32, es=esF)

    def quant_weight(idx, wt_h, nchunks, ncols, dst, entry=None):
        t_acc = entry
        for ci in range(nchunks):
            t_l = k.dma("gpsimd", wst[:, 0:ncols], wt_h[ci*128:(ci+1)*128, :],
                        waits=[t_acc] if t_acc else [])
            def ac(v, ci=ci):
                i1 = v.tensor_reduce(wabs[:, 1:2], wst[:, 0:ncols], axis=AX.X, op=ALU.add,
                                     apply_absolute_value=True)
                FENCE(k, v, i1)
                if ci == 0:
                    return v.tensor_copy(out=wabs[:, 0:1], in_=wabs[:, 1:2])
                return v.tensor_tensor(wabs[:, 0:1], wabs[:, 0:1], wabs[:, 1:2], ALU.add)
            t_acc = k.op("vector", ac, waits=[t_l])
        t_t = k.op("tensor", lambda t: t.transpose(wp_ps[0:1, 0:128], wabs[:, 0:1], ident_f[:, :]),
                   waits=[t_acc])
        def rd(v, idx=idx, nchunks=nchunks, ncols=ncols):
            i0 = v.tensor_copy(out=wred[:, :], in_=wp_ps[0:1, 0:128])
            FENCE(k, v, i0)
            i1 = v.tensor_reduce(wsc_s[0:1, idx:idx+1], wred[:, :], axis=AX.X, op=ALU.add)
            FENCE(k, v, i1)
            i2 = v.tensor_scalar(wsc_s[0:1, idx:idx+1], wsc_s[0:1, idx:idx+1], 1.0/(nchunks*128*ncols),
                                 EPS_Q, ALU.mult, ALU.max)
            FENCE(k, v, i2)
            return v.reciprocal(wsc_s[0:1, idx+2:idx+3], wsc_s[0:1, idx:idx+1])
        t_rd = k.op("vector", rd, waits=[t_t])
        def pb(g, idx=idx):
            g.partition_broadcast(wsc[:, idx:idx+1], wsc_s[0:1, idx:idx+1], channels=128)
            return g.partition_broadcast(wsc[:, idx+2:idx+3], wsc_s[0:1, idx+2:idx+3], channels=128)
        t_pb = k.op("gpsimd", pb, waits=[t_rd])
        t_q = t_pb
        for ci in range(nchunks):
            t_l2 = k.dma("gpsimd", wst[:, 0:ncols], wt_h[ci*128:(ci+1)*128, :], waits=[t_q])
            def qz(v, ci=ci, idx=idx, ncols=ncols, dst=dst):
                i1 = v.tensor_scalar(wst[:, 0:ncols], wst[:, 0:ncols], wsc[:, idx+2:idx+3], None,
                                     ALU.mult)
                FENCE(k, v, i1)
                v.tensor_scalar(dst[:, ci*ncols:(ci+1)*ncols], wst[:, 0:ncols], 0.5, None,
                                ALU.is_gt)
                i3 = v.tensor_scalar(wst[:, 0:ncols], wst[:, 0:ncols], -0.5, None, ALU.is_lt)
                FENCE(k, v, i3)
                return v.tensor_tensor(dst[:, ci*ncols:(ci+1)*ncols], dst[:, ci*ncols:(ci+1)*ncols],
                                       wst[:, 0:ncols], ALU.subtract)
            t_q = k.op("vector", qz, waits=[t_l2, t_pb])
        return t_q

    t_w1 = quant_weight(0, IN["w1t"], NCC, HID, w1q)
    t_w2 = quant_weight(1, IN["w2t"], HID//128, C, w2q, entry=t_w1)
    if FFN_DUMP:
        fd1 = nc.dram_tensor("fd_w1q", [128, NCC * HID], BF16, kind="ExternalOutput").ap()
        fd2 = nc.dram_tensor("fd_w2q", [128, (HID//128) * C], BF16, kind="ExternalOutput").ap()
        fd3 = nc.dram_tensor("fd_wsc", [1, 4], F32, kind="ExternalOutput").ap()
        k.dma("gpsimd", fd1, w1q[:, :], waits=[t_w2])
        k.dma("gpsimd", fd2, w2q[:, :], waits=[t_w2])
        k.dma("gpsimd", fd3, wsc_s[0:1, :], waits=[t_w2])

    n1t = k.sbuf("n1t", [128, C], F32, es=esF)
    n2t = k.sbuf("n2t", [128, HID], F32, es=esF)
    act_t = k.sbuf("act_t", [128, 2], F32, es=esF)
    t_c1 = k.dma_multi("gpsimd", [(n1t[0:1, :], IN["norm1_w"]), (n2t[0:1, :], IN["norm2_w"]),
                                  (act_t[0:1, :], IN["actsc"])])
    def cb(g):
        g.partition_broadcast(n1t[:, :], n1t[0:1, :], channels=128)
        g.partition_broadcast(n2t[:, :], n2t[0:1, :], channels=128)
        return g.partition_broadcast(act_t[:, :], act_t[0:1, :], channels=128)
    t_cb = k.op("gpsimd", cb, waits=[t_c1])

    scr = k.sbuf("f_scr", [128, 16], F32, es=esF)
    x2t = k.sbuf("f_x2", [128, C], F32, es=esF)
    xqb = k.sbuf("f_xqb", [128, C], BF16, es=esF)
    xqT = k.sbuf("f_xqT", [128, C], BF16, es=esF)
    hfp = k.sbuf("f_hfp", [128, HID], F32, es=esF)
    hqb = k.sbuf("f_hqb", [128, HID], BF16, es=esF)
    hqT = k.sbuf("f_hqT", [128, HID], BF16, es=esF)
    tpq = k.psum("f_tpq", [128, 512], F32, es=esF)
    fmp = [k.psum(f"f_mp{i}", [128, 512], F32, es=esF) for i in range(2)]

    t_iter = [t_w1, t_w2, t_cb]
    for s in range(SEQ):
        for rc in range(RC):
            t_x = k.dma("gpsimd", x2t[:, :], x2s[s, rc*128:(rc+1)*128, :], waits=t_iter)
            t_wr_prev = t_iter[0] if t_iter else None
            t_iter = []
            tln = hfp[:, 1024:2048]
            t_ln, _ = seq_ln(k, x2t, [tln], hfp[:, 2048:3072], scr, [t_x])
            def rr(v):
                i1 = v.tensor_scalar(scr[:, 4:5], scr[:, 1:2], EPS_LN, None, ALU.add)
                FENCE(k, v, i1)
                i2 = v.reciprocal(scr[:, 5:6], scr[:, 4:5])
                FENCE(k, v, i2)
                i3 = v.tensor_tensor(scr[:, 6:7], scr[:, 1:2], scr[:, 5:6], ALU.mult)
                FENCE(k, v, i3)
                i4 = v.tensor_scalar(scr[:, 6:7], scr[:, 6:7], EPS_RMS, None, ALU.add)
                FENCE(k, v, i4)
                return v.reciprocal(scr[:, 7:8], scr[:, 6:7])
            t_rr = k.op("vector", rr, waits=[t_ln])
            t_rs = k.op("scalar", lambda s_: s_.activation(scr[:, 4:5], scr[:, 7:8], AF.Sqrt), waits=[t_rr])
            def xq(v):
                i0 = v.tensor_tensor(tln, tln, n1t[:, :], ALU.mult)
                FENCE(k, v, i0)
                i1 = v.tensor_reduce(scr[:, 8:9], tln, axis=AX.X, op=ALU.max, apply_absolute_value=True)
                FENCE(k, v, i1)
                i2 = v.reciprocal(scr[:, 9:10], scr[:, 8:9])
                v.tensor_tensor(scr[:, 10:11], scr[:, 4:5], scr[:, 8:9], ALU.mult)
                i3 = v.tensor_scalar(scr[:, 9:10], scr[:, 9:10], 127.0, None, ALU.mult)
                i4 = v.tensor_scalar(scr[:, 10:11], scr[:, 10:11], EPS_Q, 1.0/127.0, ALU.max, ALU.mult)
                FENCE(k, v, i4)
                return v.tensor_tensor(scr[:, 11:12], scr[:, 10:11], wsc[:, 0:1], ALU.mult)
            t_xq = k.op("vector", xq, waits=[t_rs, t_cb])
            t_xr = seq_round_to(k, tln, xqb[:, :], scr[:, 9:10], [t_xq])
            t_xT = seq_transpose(k, xqb, list(range(8)), lambda j: xqT[:, j*128:j*128+128],
                                 ident_b[:, :], tpq, fmp[0], [t_xr])
            t_evh = []
            for fb in range(8):
                pb = fmp[fb % 2]
                w_mm = [t_xT, t_w1] + ([t_evh[-2]] if len(t_evh) >= 2 else [])
                t_mm = seq_mm(k, pb[:, :],
                              [xqT[:, cc*128:(cc+1)*128] for cc in range(NCC)],
                              [w1q[:, cc*HID + fb*512: cc*HID + fb*512 + 512] for cc in range(NCC)],
                              w_mm)
                def ev(s_, fb=fb, pb=pb):
                    return s_.activation(hfp[:, fb*512:(fb+1)*512], pb[:, :], AF.Relu, scale=scr[:, 11:12])
                t_evh.append(k.op("scalar", ev,
                                  waits=[t_mm, t_xq] + ([t_wr_prev] if fb < 2 and t_wr_prev else [])))
            t_sq = k.op("scalar", lambda s_: s_.activation(hfp[:, :], hfp[:, :], AF.Square),
                        waits=[t_evh[-1]])
            t_ab = k.op("vector", lambda v: v.tensor_scalar(hfp[:, :], hfp[:, :], act_t[:, 0:1],
                                                            act_t[:, 1:2], ALU.mult, ALU.add),
                        waits=[t_sq])
            def h2s(v):
                for i in range(4):
                    v.affine_mul_reduce(xqT[:, 0:1024], scr[:, i:i+1],
                                        hfp[:, i*1024:(i+1)*1024], hfp[:, i*1024:(i+1)*1024], 1.0, 0.0)
                v.tensor_tensor(scr[:, 0:1], scr[:, 0:1], scr[:, 1:2], ALU.add)
                i1 = v.tensor_tensor(scr[:, 2:3], scr[:, 2:3], scr[:, 3:4], ALU.add)
                FENCE(k, v, i1)
                i2 = v.tensor_tensor(scr[:, 2:3], scr[:, 0:1], scr[:, 2:3], ALU.add)
                FENCE(k, v, i2)
                i3 = v.tensor_scalar(scr[:, 2:3], scr[:, 2:3], 1.0/HID, EPS_RMS, ALU.mult, ALU.add)
                FENCE(k, v, i3)
                return v.reciprocal(scr[:, 3:4], scr[:, 2:3])
            t_h2s = k.op("vector", h2s, waits=[t_ab])
            t_h2r = k.op("scalar", lambda s_: s_.activation(scr[:, 4:5], scr[:, 3:4], AF.Sqrt),
                         waits=[t_h2s])
            def h2q(v):
                i0 = v.tensor_tensor(hfp[:, :], hfp[:, :], n2t[:, :], ALU.mult)
                FENCE(k, v, i0)
                i1 = v.tensor_reduce(scr[:, 8:9], hfp[:, :], axis=AX.X, op=ALU.max, apply_absolute_value=True)
                FENCE(k, v, i1)
                i2 = v.reciprocal(scr[:, 9:10], scr[:, 8:9])
                v.tensor_tensor(scr[:, 10:11], scr[:, 4:5], scr[:, 8:9], ALU.mult)
                i3 = v.tensor_scalar(scr[:, 9:10], scr[:, 9:10], 127.0, None, ALU.mult)
                i4 = v.tensor_scalar(scr[:, 10:11], scr[:, 10:11], EPS_Q, 1.0/127.0, ALU.max, ALU.mult)
                FENCE(k, v, i4)
                return v.tensor_tensor(scr[:, 11:12], scr[:, 10:11], wsc[:, 1:2], ALU.mult)
            t_hq = k.op("vector", h2q, waits=[t_h2r])
            t_hr = seq_round_to(k, hfp[:, :], hqb[:, :], scr[:, 9:10], [t_hq])
            t_hT = seq_transpose(k, hqb, list(range(32)), lambda j: hqT[:, j*128:j*128+128],
                                 ident_b[:, :], tpq, fmp[0], [t_hr])
            t_evy = []
            for fb in range(2):
                pb = fmp[fb % 2]
                t_mm = seq_mm(k, pb[:, :],
                              [hqT[:, hc*128:(hc+1)*128] for hc in range(HID//128)],
                              [w2q[:, hc*C + fb*512: hc*C + fb*512 + 512] for hc in range(HID//128)],
                              [t_hT, t_w2] + t_evy)
                def evy(v, fb=fb, pb=pb):
                    return v.scalar_tensor_tensor(hfp[:, fb*512:(fb+1)*512], pb[:, :], scr[:, 11:12],
                                                  x2t[:, fb*512:(fb+1)*512], ALU.mult, ALU.add)
                t_evy.append(k.op("vector", evy, waits=[t_mm, t_hq]))
            t_wr = k.dma("gpsimd", x3_out[s, rc*128:(rc+1)*128, :], hfp[:, 0:1024], waits=t_evy)
            if FFN_DUMP:
                t_d1 = k.dma("sync", fscr_d[s*RC + rc], scr[:, :], waits=t_evy)
                t_d2 = k.dma("sync", fxq_d[s*RC + rc], xqb[:, :], waits=[t_hr])
                t_d3 = k.dma("sync", fhq_d[s*RC + rc], hqb[:, :], waits=[t_hr])
                t_iter = [t_wr, t_d1, t_d2, t_d3]
            else:
                t_iter = [t_wr]
    k.wait_only("gpsimd", k.dma_drain("gpsimd") + k.dma_drain("sync") + t_iter)
    k.realize("l0f")
    esF.close()



GP = 32
NG = 256 // GP
GR = 16 * GP
NRC2 = GR // 128
F_SLOW = 3088
NT = 64
NB = 4


def build_phaseB1(k, IN, x3g_src, x4s, es_hold, zmax_out=None):
    nc = k.nc
    esB = es_hold
    zmx = k.sbuf("B_zmx", [128, 1], F32, es=esB) if zmax_out is not None else None
    zpr = k.sbuf("B_zpr", [128, 1024], F32, es=esB) if zmax_out is not None else None
    ident_b = k.sbuf("B_identb", [128, 128], BF16, es=esB)
    ident_f = k.sbuf("B_identf", [128, 128], F32, es=esB)
    outT = k.sbuf("B_outT", [128, 8 * GR], BF16, es=esB)
    def cinit(g):
        g.memset(ident_b[:, :], 1.0)
        g.memset(ident_f[:, :], 1.0)
        g.memset(outT[:, :], 0.0)
        g.affine_select(ident_b[:, :], ident_b[:, :], pattern=[[1, 128]], compare_op=ALU.is_equal,
                        fill=0.0, base=0, channel_multiplier=-1)
        return g.affine_select(ident_f[:, :], ident_f[:, :], pattern=[[1, 128]], compare_op=ALU.is_equal,
                               fill=0.0, base=0, channel_multiplier=-1)
    t_const = k.op("gpsimd", cinit)

    wslow_b = k.sbuf("wslow_b", [128, NCC * F_SLOW], BF16, es=esB)
    wout_b = k.sbuf("wout_b", [128, NCC * 1024], BF16, es=esB)
    t_ws = k.dma_multi("gpsimd", [(wslow_b[:, cc*F_SLOW:(cc+1)*F_SLOW], IN["w_slow_p"][cc*128:(cc+1)*128, :])
                                  for cc in range(NCC)])
    t_wo = k.dma_multi("gpsimd", [(wout_b[:, cc*1024:(cc+1)*1024], IN["w_out"][cc*128:(cc+1)*128, :])
                                  for cc in range(NCC)])
    lnw_t = k.sbuf("lnw_t", [128, C], F32, es=esB)
    lnb_t = k.sbuf("lnb_t", [128, C], F32, es=esB)
    msk = k.sbuf("msk", [128, 384], BF16, es=esB)
    t_c2 = k.dma_multi("gpsimd", [(lnw_t[0:1, :], IN["ln_w"]), (lnb_t[0:1, :], IN["ln_b"]),
                                  (msk[:, :], IN["masks"])])
    def cb(g):
        g.partition_broadcast(lnw_t[:, :], lnw_t[0:1, :], channels=128)
        return g.partition_broadcast(lnb_t[:, :], lnb_t[0:1, :], channels=128)
    t_cb = k.op("gpsimd", cb, waits=[t_c2])

    y3g = k.sbuf("y3g", [128, NRC2 * 1024], F32, es=esB)
    hbt = k.sbuf("hbt", [128, 1024], BF16, es=esB)
    hT = k.sbuf("hT", [128, 1024], BF16, es=esB)
    qkvb = k.sbuf("qkvb", [128, F_SLOW], F32, es=esB)
    xio = k.sbuf("B_xio", [128, 1024], F32, es=esB)
    scr = k.sbuf("B_scr", [128, 16], F32, es=esB)
    sm16 = k.sbuf("sm16", [128, 48], F32, es=esB)
    qnT_s = k.sbuf("qnT_s", [128, 8 * GR], BF16, es=esB)
    knT_s = k.sbuf("knT_s", [128, 8 * GR], BF16, es=esB)
    kbT_s = k.sbuf("kbT_s", [128, 8 * GR], BF16, es=esB)
    cT_s = k.sbuf("cT_s", [128, 8 * GR], BF16, es=esB)
    Xb = k.sbuf("Xb", [128, 2048], BF16, es=esB)
    Y1b = k.sbuf("Y1b", [128, 2048], BF16, es=esB)
    Y2b = k.sbuf("Y2b", [128, 2048], BF16, es=esB)
    Y4b = k.sbuf("Y4b", [128, 2048], BF16, es=esB)
    Y8b = k.sbuf("Y8b", [128, 2048], BF16, es=esB)
    X2b = k.sbuf("X2b", [128, 2048], BF16, es=esB)
    Pb_ = k.sbuf("Pb_", [128, 2048], BF16, es=esB)
    Ub = [k.sbuf(f"Ub{i}", [128, 1024], F32, es=esB) for i in range(2)]
    Ubb = k.sbuf("Ubb", [128, 1024], BF16, es=esB)
    Ob = k.sbuf("Ob", [128, 1024], BF16, es=esB)

    mp = [k.psum(f"B_mp{i}", [128, 512], F32, es=esB) for i in range(2)]
    rp = [k.psum(f"B_rp{i}", [128, 512], F32, es=esB) for i in range(4)]
    up_ = k.psum("B_up", [128, 512], F32, es=esB)
    tp_ = k.psum("B_tp", [128, 512], F32, es=esB)

    def sl_src(tile, h, pg):
        base = (h % 2) * 64
        off = (h // 2) * GR + pg * 128
        return tile[base:base+64, off:off+128]

    t_grp_end = None
    t_wout_done = []
    t_zm_last = []
    for g in range(NG):
        t_stk = None
        t_rc_tok = None
        for rc2 in range(NRC2):
            w0 = ([t_grp_end] if (t_grp_end and rc2 == 0) else [])
            guard = [t_rc_tok] if t_rc_tok else []
            prs = [(xio[tt_*32:(tt_+1)*32, :], x3g_src[rc2*4 + tt_, g*GP:(g+1)*GP, :]) for tt_ in range(4)]
            t_x = k.dma_multi("sync", prs, waits=w0 + guard)
            y3sl = y3g[:, rc2*1024:(rc2+1)*1024]
            t_ln3, _ = seq_ln(k, xio, [y3sl], qkvb[:, 0:1024], scr, [t_x])
            t_ln4, _ = seq_ln(k, y3sl, [xio[:, :]], qkvb[:, 0:1024], scr, [t_ln3], w_extra=[t_cb])
            def aff(v):
                i1 = v.tensor_tensor(xio[:, :], xio[:, :], lnw_t[:, :], ALU.mult)
                FENCE(k, v, i1)
                return v.tensor_tensor(hbt[:, :], xio[:, :], lnb_t[:, :], ALU.add)
            t_aff = k.op("vector", aff, waits=[t_ln4])
            t_hT = seq_transpose(k, hbt, list(range(8)), lambda j: hT[:, j*128:j*128+128],
                                 ident_b[:, :], tp_, rp[0], [t_aff, t_const], evac_eng="scalar")
            t_ev = []
            for fb in range(7):
                w_ = 512 if fb < 6 else 16
                pb = mp[fb % 2]
                w_mm = [t_hT, t_ws] + ([t_ev[-2]] if len(t_ev) >= 2 else [])
                t_mm = seq_mm(k, pb[:, 0:w_],
                              [hT[:, cc*128:(cc+1)*128] for cc in range(NCC)],
                              [wslow_b[:, cc*F_SLOW + fb*512: cc*F_SLOW + fb*512 + w_] for cc in range(NCC)],
                              w_mm)
                def ef(e, pb=pb, fb=fb, w_=w_):
                    return e.copy(out=qkvb[:, fb*512:fb*512+w_], in_=pb[:, 0:w_])
                t_ev.append(k.op("scalar", ef, waits=[t_mm]))
            t_qkvb = t_ev[-1]
            t_beta = k.op("scalar", lambda s_: s_.activation(sm16[:, 32:48], qkvb[:, 3072:3088], AF.Sigmoid),
                          waits=[t_qkvb])
            def cf(v):
                bc = sm16[:, 32:48].unsqueeze(2).broadcast_to([128, 16, 64])
                q3 = qkvb[:, 2048:3072].rearrange("p (h d) -> p h d", h=16)
                return v.tensor_tensor(q3, q3, bc, ALU.mult)
            t_c = k.op("vector", cf, waits=[t_beta, t_qkvb])
            toks = {}
            for nm, lo in (("q", 0), ("k", 1024)):
                def mn(v, lo=lo):
                    return v.tensor_scalar(xio[:, :], qkvb[:, lo:lo+1024], 0.0, None, ALU.min)
                t1 = k.op("vector", mn, waits=[t_qkvb] + ([toks["q"]] if nm == "k" else [t_c]))
                t2 = k.op("scalar", lambda s_: s_.activation(xio[:, :], xio[:, :], AF.Exp), waits=[t1])
                def mx(v, lo=lo):
                    return v.scalar_tensor_tensor(qkvb[:, lo:lo+1024], qkvb[:, lo:lo+1024], 1.0,
                                                  xio[:, :], ALU.add, ALU.max)
                t3 = k.op("vector", mx, waits=[t2])
                def sn(v, lo=lo):
                    q3 = qkvb[:, lo:lo+1024].rearrange("p (h d) -> p h d", h=16)
                    i1 = v.tensor_reduce(sm16[:, 0:16], q3, axis=AX.X, op=ALU.add)
                    FENCE(k, v, i1)
                    i2 = v.reciprocal(sm16[:, 16:32], sm16[:, 0:16])
                    FENCE(k, v, i2)
                    bc = sm16[:, 16:32].unsqueeze(2).broadcast_to([128, 16, 64])
                    return v.tensor_tensor(q3, q3, bc, ALU.mult)
                toks[nm] = k.op("vector", sn, waits=[t3])
            def kbf(v):
                bc = sm16[:, 32:48].unsqueeze(2).broadcast_to([128, 16, 64])
                k3 = qkvb[:, 1024:2048].rearrange("p (h d) -> p h d", h=16)
                x3_ = xio[:, :].rearrange("p (h d) -> p h d", h=16)
                return v.tensor_tensor(x3_, k3, bc, ALU.mult)
            t_kb = k.op("vector", kbf, waits=[toks["k"]])
            def stk_dst(tile, hp, rc2):
                return tile[:, :].rearrange("q (a b c) -> q a b c", a=8, b=4)[:, hp, :, :] if False else None
            t_tr = []
            for src_t, dst_t, tok in ((qkvb[:, 0:1024], qnT_s, toks["q"]),
                                      (qkvb[:, 1024:2048], knT_s, toks["k"]),
                                      (xio[:, :], kbT_s, t_kb),
                                      (qkvb[:, 2048:3072], cT_s, t_c)):
                for hp in range(8):
                    def tf(t, src_t=src_t, hp=hp):
                        return t.transpose(tp_[:, 0:128], src_t[:, hp*128:(hp+1)*128], ident_f[:, :])
                    t_t = k.op("tensor", tf, waits=[tok, t_const] + (t_tr[-1:] if t_tr else [t_hT]))
                    def ef2(e, dst_t=dst_t, hp=hp, rc2=rc2):
                        dst = dst_t[:, :].rearrange("q (a p t) -> q a p t", a=8, p=GP)[:, hp, :, rc2*4:(rc2+1)*4]
                        s_ = tp_[:, 0:128].rearrange("q (t p) -> q p t", t=4)
                        return e.tensor_copy(out=dst, in_=s_)
                    t_tr.append(k.op("vector", ef2, waits=[t_t]))
            t_rc_tok = t_tr[-1]
            t_stk = t_tr[-1]
        t_stk_all = [t_stk]

        t_out_evac = None
        batch_guard = [None]*4
        for b in range(NB):
            tiles = [(h, pg) for h in range(b*4, b*4+4) for pg in range(4)]
            def round_mms(lhs_fn, rhs_fn, dst_tile, mask_col, waits, evac_eng="vector"):
                toks_ = []
                for bi in range(4):
                    def tf(t, bi=bi, tiles=tiles):
                        last = None
                        for j in range(4):
                            ti = bi*4 + j
                            h, pg = tiles[ti]
                            last = t.matmul(rp[bi][:, j*128:(j+1)*128], lhs_fn(h, pg), rhs_fn(h, pg),
                                            start=True, stop=True)
                        return last
                    toks_.append(k.op("tensor", tf, waits=list(waits) + ([t_ev_r[bi]] if t_ev_r and t_ev_r[bi] else [])))
                evs = []
                for bi in range(4):
                    def ef(e, bi=bi):
                        dst = dst_tile[:, bi*512:(bi+1)*512]
                        if mask_col is None:
                            return e.tensor_copy(out=dst, in_=rp[bi][:, :])
                        m3 = msk[:, mask_col:mask_col+128].unsqueeze(1).broadcast_to([128, 4, 128])
                        d3 = dst.rearrange("p (a b) -> p a b", a=4)
                        s3 = rp[bi][:, :].rearrange("p (a b) -> p a b", a=4)
                        return e.tensor_tensor(d3, s3, m3, ALU.mult)
                    evs.append(k.op(evac_eng, ef, waits=[toks_[bi]] + toks_[bi+1:bi+2]))
                return evs
            t_ev_r = batch_guard
            w0 = t_stk_all + [t_ws]
            ev = round_mms(lambda h, pg: sl_src(kbT_s, h, pg), lambda h, pg: sl_src(knT_s, h, pg),
                           Xb, 0, w0)
            t_ev_r = ev
            ev1 = round_mms(lambda h, pg: sl_src(knT_s, h, pg), lambda h, pg: sl_src(kbT_s, h, pg),
                            Y1b, 128, w0)
            t_ev_r = ev1
            def bsl(tile, ti):
                return tile[:, ti*128:(ti+1)*128]
            def tile_lhs(tile, tiles=tiles):
                return lambda h, pg, tiles=tiles: bsl(tile, tiles.index((h, pg)))
            ev2 = round_mms(tile_lhs(Xb), tile_lhs(Y1b), Y2b, None, [ev[-1], ev1[-1]])
            t_ev_r = ev2
            ev3 = round_mms(tile_lhs(Y1b), tile_lhs(Xb), X2b, None, [ev[-1], ev1[-1]])
            t_ev_r = ev3
            ev4 = round_mms(tile_lhs(X2b), tile_lhs(Y2b), Y4b, None, [ev2[-1], ev3[-1]])
            t_ev_r = ev4
            ev5 = round_mms(tile_lhs(Y2b), tile_lhs(X2b), X2b, None, [ev2[-1], ev3[-1]])
            t_ev_r = ev5
            ev6 = round_mms(tile_lhs(X2b), tile_lhs(Y4b), Y8b, None, [ev4[-1], ev5[-1]])
            t_ev_r = ev6
            ev7 = round_mms(lambda h, pg: sl_src(knT_s, h, pg), lambda h, pg: sl_src(qnT_s, h, pg),
                            Pb_, 256, w0)
            upb = up_[:, :].bitcast(BF16)
            t_cst = []
            for bi in range(4):
                def tf(t, bi=bi, tiles=tiles):
                    last = None
                    for j in range(4):
                        h, pg = tiles[bi*4 + j]
                        base = (h % 2) * 64
                        last = t.transpose(upb[:, j*64:(j+1)*64], sl_src(cT_s, h, pg),
                                           ident_b[base:base+64, base:base+64])
                    return last
                tt_ = k.op("tensor", tf, waits=t_stk_all + [ev7[-1]] + (t_cst[-1:] if t_cst else []))
                def ef(e, bi=bi):
                    e.tensor_copy(out=Ub[0][:, bi*256:(bi+1)*256], in_=upb[:, 0:256])
                    return e.tensor_copy(out=Ubb[:, bi*256:(bi+1)*256], in_=upb[:, 0:256])
                t_cst.append(k.op("vector", ef, waits=[tt_]))
            cur = 0
            t_uprev = t_cst[-1]
            for yb, wy in ((Y8b, ev6), (Y4b, ev4), (Y2b, ev2), (Y1b, ev1)):
                t_um = []
                for bi in range(4):
                    def tf(t, bi=bi, yb=yb):
                        last = None
                        for j in range(4):
                            ti = bi*4 + j
                            last = t.matmul(rp[bi][:, j*128:j*128+64],
                                            bsl(yb, ti), Ubb[:, ti*64:(ti+1)*64], start=True, stop=True)
                        return last
                    t_um.append(k.op("tensor", tf, waits=[t_uprev, wy[-1]]))
                t_ua = []
                for bi in range(4):
                    def ef(v, bi=bi, cur=cur):
                        s3 = rp[bi][:, :].rearrange("p (a b) -> p a b", a=4)[:, :, 0:64]
                        i0 = Ub[cur][:, bi*256:(bi+1)*256].rearrange("p (a b) -> p a b", a=4)
                        o0 = Ub[1-cur][:, bi*256:(bi+1)*256].rearrange("p (a b) -> p a b", a=4)
                        i1 = v.tensor_tensor(o0, s3, i0, ALU.add)
                        FENCE(k, v, i1)
                        return v.tensor_copy(out=Ubb[:, bi*256:(bi+1)*256],
                                             in_=Ub[1-cur][:, bi*256:(bi+1)*256])
                    t_ua.append(k.op("vector", ef, waits=[t_um[bi]]))
                t_uprev = t_ua[-1]
                cur = 1 - cur
            t_om = []
            for bi in range(4):
                def tf(t, bi=bi):
                    last = None
                    for j in range(4):
                        ti = bi*4 + j
                        last = t.matmul(rp[bi][:, j*128:j*128+64], bsl(Pb_, ti),
                                        Ubb[:, ti*64:(ti+1)*64], start=True, stop=True)
                    return last
                t_om.append(k.op("tensor", tf, waits=[t_uprev, ev7[-1]]))
            t_oe = []
            for bi in range(4):
                def ef(v, bi=bi):
                    s3 = rp[bi][:, :].rearrange("p (a b) -> p a b", a=4)[:, :, 0:64]
                    d3 = Ob[:, bi*256:(bi+1)*256].rearrange("p (a b) -> p a b", a=4)
                    return v.tensor_copy(out=d3, in_=s3)
                t_oe.append(k.op("vector", ef, waits=[t_om[bi]]))
            t_ot = []
            for bi in range(4):
                def tf(t, bi=bi):
                    last = None
                    for j in range(4):
                        ti = bi*4 + j
                        last = t.transpose(upb[0:64, j*128:(j+1)*128], Ob[:, ti*64:(ti+1)*64], ident_b[:, :])
                    return last
                tt_ = k.op("tensor", tf, waits=[t_oe[bi]] + (t_ot[-1:] if t_ot else [t_cst[-1]]))
                def ef(e, bi=bi, b=b, tiles=tiles):
                    last = None
                    for j in range(4):
                        ti = bi*4 + j
                        h, pg = tiles[ti]
                        base = (h % 2) * 64
                        dst = outT[base:base+64, :].rearrange("q (a t p) -> q a t p", a=8, t=16)[:, h//2, :, pg*8:(pg+1)*8]
                        s_ = upb[0:64, j*128:(j+1)*128].rearrange("q (p t) -> q t p", p=8)
                        last = _copy(e, dst, s_)
                    return last
                t_ot.append(k.op("scalar", ef, waits=[tt_]))
            t_out_evac = t_ot[-1]
            batch_guard = [t_ot[-1]] * 4
        t_x4 = []
        for rc2 in range(NRC2):
            prs = [(xio[tt_*32:(tt_+1)*32, :], x3g_src[rc2*4 + tt_, g*GP:(g+1)*GP, :]) for tt_ in range(4)]
            t_rx = k.dma_multi("sync", prs, waits=[t_out_evac] + t_x4[-1:] + t_zm_last)
            t_pe = []
            for fb in range(2):
                pb = mp[fb % 2]
                t_mm = seq_mm(k, pb[:, :],
                              [outT[:, :].rearrange("q (a r) -> q a r", a=8)[:, cc, rc2*128:(rc2+1)*128]
                               for cc in range(NCC)],
                              [wout_b[:, cc*1024 + fb*512: cc*1024 + fb*512 + 512] for cc in range(NCC)],
                              [t_out_evac, t_wo] + t_pe + t_x4[-1:])
                def epf(v, fb=fb, pb=pb, rc2=rc2):
                    i1 = v.tensor_tensor(xio[:, fb*512:(fb+1)*512], xio[:, fb*512:(fb+1)*512],
                                         y3g[:, rc2*1024 + fb*512: rc2*1024 + fb*512 + 512], ALU.add)
                    FENCE(k, v, i1)
                    return v.tensor_tensor(xio[:, fb*512:(fb+1)*512], xio[:, fb*512:(fb+1)*512],
                                           pb[:, :], ALU.add)
                t_pe.append(k.op("vector", epf, waits=[t_mm, t_rx]))
            prs2 = [(x4s[rc2*4 + tt_, g*GP:(g+1)*GP, :], xio[tt_*32:(tt_+1)*32, :]) for tt_ in range(4)]
            t_wr = k.dma_multi("gpsimd", prs2, waits=t_pe)
            t_x4.append(t_wr)
            if zmax_out is not None:
                zsc = y3g[:, rc2*1024:(rc2+1)*1024]
                t_z5, _ = seq_ln(k, xio, [zsc], zpr[:, :], scr, t_pe)
                first = (g == 0 and rc2 == 0)
                def zmf(v, first=first, zsc=zsc):
                    i = v.tensor_reduce(scr[:, 8:9], zsc, axis=AX.X, op=ALU.max,
                                        apply_absolute_value=True)
                    if first:
                        return v.tensor_copy(out=zmx[:, :], in_=scr[:, 8:9])
                    return v.tensor_tensor(zmx[:, :], zmx[:, :], scr[:, 8:9], ALU.max)
                t_zm_last = [k.op("vector", zmf, waits=[t_z5])]
        t_grp_end = t_x4[-1]
    if zmax_out is not None:
        t_zt = k.op("tensor", lambda t: t.transpose(tp_[0:1, 0:128], zmx[:, :], ident_f[:, :]),
                    waits=t_zm_last)
        def zrf(v):
            v.tensor_copy(out=y3g[0:1, 0:128], in_=tp_[0:1, 0:128])
            return v.tensor_reduce(y3g[0:1, 128:129], y3g[0:1, 0:128], axis=AX.X, op=ALU.max)
        t_zr = k.op("vector", zrf, waits=[t_zt])
        t_zd = k.dma("gpsimd", zmax_out, y3g[0:1, 128:129], waits=[t_zr])
        t_x4 = t_x4 + [t_zd]
    return t_x4


HID2 = 3072
NOC = HID2 // 128
NCH = 32
I16 = mybir.dt.int16


def chunk_src(x4s, ci):
    t = ci // 2
    p0 = (ci % 2) * 128
    return x4s[t, p0:p0+128, :]


def gmax_reduce(k, IN_gmax, kvals, dst_bcast, es, tag, scale=1.0/32767.0, floor=1e-12,
                recip_col=True):
    gt = k.sbuf(f"gm_{tag}", [1, 8 * kvals], F32, es=es)
    st = k.sbuf(f"gs_{tag}", [1, 2 * kvals], F32, es=es)
    t_l = k.dma("gpsimd", gt[0:1, :], IN_gmax)
    def rf(v):
        g3 = gt[0:1, :].rearrange("q (c kk) -> q kk c", c=8)
        v.tensor_reduce(st[0:1, 0:kvals], g3, axis=AX.X, op=ALU.max)
        i = v.tensor_scalar(st[0:1, 0:kvals], st[0:1, 0:kvals], scale, floor,
                            ALU.mult, ALU.max)
        if not recip_col:
            return i
        return v.reciprocal(st[0:1, kvals:2*kvals], st[0:1, 0:kvals])
    t_r = k.op("vector", rf, waits=[t_l])
    def bf(g):
        return g.partition_broadcast(dst_bcast[:, :], st[0:1, 0:(2 * kvals if recip_col else kvals)],
                                     channels=128)
    return k.op("gpsimd", bf, waits=[t_r])


def build_lpbq(k, w_nat, nrows, ncols, wq_nat, es, tag, entry=None):
    nb = ncols // 32
    wst = k.sbuf(f"lq{tag}_st", [128, ncols], F32, es=es)
    wsc_ = k.sbuf(f"lq{tag}_sc", [128, 2 * nb], F32, es=es)
    wqb = k.sbuf(f"lq{tag}_q", [128, ncols], BF16, es=es)
    t_prev = entry
    for ci in range(nrows // 128):
        t_l = k.dma("gpsimd", wst[:, :], w_nat[ci*128:(ci+1)*128, :],
                    waits=[t_prev] if t_prev else [])
        w3 = wst[:, :].rearrange("p (b e) -> p b e", e=32)
        def q1(v):
            v.tensor_reduce(wsc_[:, 0:nb], w3, axis=AX.X, op=ALU.max, apply_absolute_value=True)
            v.tensor_scalar(wsc_[:, 0:nb], wsc_[:, 0:nb], 1.0/7.0, 1e-12, ALU.mult, ALU.max)
            v.reciprocal(wsc_[:, nb:2*nb], wsc_[:, 0:nb])
            isb = wsc_[:, nb:2*nb].unsqueeze(2).broadcast_to([128, nb, 32])
            return v.tensor_tensor(w3, w3, isb, ALU.mult)
        t_a = k.op("vector", q1, waits=[t_l])
        def q2(v):
            v.tensor_scalar(wst[:, :], wst[:, :], MAGIC, None, ALU.add)
            v.tensor_scalar(wst[:, :], wst[:, :], -MAGIC, -8.0, ALU.add, ALU.max)
            v.tensor_scalar(wst[:, :], wst[:, :], 7.0, None, ALU.min)
            sb = wsc_[:, 0:nb].unsqueeze(2).broadcast_to([128, nb, 32])
            q3b = wqb[:, :].rearrange("p (b e) -> p b e", e=32)
            return v.tensor_tensor(q3b, w3, sb, ALU.mult)
        t_q = k.op("vector", q2, waits=[t_a])
        t_prev = k.dma("gpsimd", wq_nat[ci*128:(ci+1)*128, :], wqb[:, :], waits=[t_q])
    return t_prev, wqb


def load_wT(k, wq_nat, nrows, ncols, dst, stage, ident_b, tps, tps2, es, waits, tag):
    t_prev = list(waits)
    t_ev = None
    for oc in range(nrows // 128):
        t_l = k.dma("sync", stage[:, :], wq_nat[oc*128:(oc+1)*128, :],
                    waits=t_prev + ([t_ev] if t_ev else []))
        t_prev = []
        t_ev = seq_transpose(k, stage, list(range(ncols // 128)),
                             lambda j, oc=oc: dst[:, j*nrows + oc*128: j*nrows + oc*128 + 128],
                             ident_b[:, :], tps, tps2, [t_l] + ([t_ev] if t_ev else []),
                             evac_eng="scalar")
    return t_ev


def build_P0(k, IN, es):
    nc = k.nc
    x4s = IN["x4s"]
    ident_b = k.sbuf("p0_identb", [128, 128], BF16, es=es)
    ident_f = k.sbuf("p0_identf", [128, 128], F32, es=es)
    def cinit(g):
        g.memset(ident_b[:, :], 1.0)
        g.memset(ident_f[:, :], 1.0)
        g.affine_select(ident_b[:, :], ident_b[:, :], pattern=[[1, 128]], compare_op=ALU.is_equal,
                        fill=0.0, base=0, channel_multiplier=-1)
        return g.affine_select(ident_f[:, :], ident_f[:, :], pattern=[[1, 128]], compare_op=ALU.is_equal,
                               fill=0.0, base=0, channel_multiplier=-1)
    t_const = k.op("gpsimd", cinit)

    s1b = k.sbuf("p0_s1b", [128, 2], F32, es=es)
    t_s1 = gmax_reduce(k, IN["gmax_z"], 1, s1b, es, "s1")

    wuq_nat = nc.dram_tensor("wuq_nat", [HID2, C], BF16, kind="Internal").ap()
    wgq_nat = nc.dram_tensor("wgq_nat", [HID2, C], BF16, kind="Internal").ap()
    t_qu, wqb_u = build_lpbq(k, IN["w_up"], HID2, C, wuq_nat, es, "u")
    t_qg, wqb_g = build_lpbq(k, IN["w_gate"], HID2, C, wgq_nat, es, "g", entry=t_qu)

    wuT = k.sbuf("p0_wuT", [128, NCC * HID2], BF16, es=es)
    wgT = k.sbuf("p0_wgT", [128, NCC * HID2], BF16, es=es)
    tps = k.psum("p0_tps", [128, 512], F32, es=es)
    mp = [k.psum(f"p0_mp{i}", [128, 512], F32, es=es) for i in range(4)]
    t_wu = load_wT(k, wuq_nat, HID2, C, wuT, wqb_u, ident_b, tps, mp[3], es, [t_qu, t_const], "u")
    t_wg = load_wT(k, wgq_nat, HID2, C, wgT, wqb_g, ident_b, tps, mp[3], es, [t_qg, t_wu], "g")

    x4t = k.sbuf("p0_x4", [128, C], F32, es=es)
    zst = k.sbuf("p0_z", [128, C], F32, es=es)
    zpr = k.sbuf("p0_zp", [128, C], F32, es=es)
    scr = k.sbuf("p0_scr", [128, 16], F32, es=es)
    zqb = k.sbuf("p0_zqb", [128, C], BF16, es=es)
    zqT = k.sbuf("p0_zqT", [128, C], BF16, es=es)
    ust = k.sbuf("p0_ust", [128, HID2], F32, es=es)
    gst = k.sbuf("p0_gst", [128, HID2], F32, es=es)
    mxa = k.sbuf("p0_mx", [128, 2], F32, es=es)

    up_raw = nc.dram_tensor("up_raw", [NCH, 128, HID2], F32, kind="ExternalOutput").ap()
    gate_raw = nc.dram_tensor("gate_raw", [NCH, 128, HID2], F32, kind="ExternalOutput").ap()
    mx_out = nc.dram_tensor("gmax_ug", [1, 2], F32, kind="ExternalOutput").ap()

    t_iter = [t_wg, t_s1]
    t_wr_u = t_wr_g = None
    t_mx = None
    for ci in range(NCH):
        t_x = k.dma("gpsimd", x4t[:, :], chunk_src(x4s, ci), waits=t_iter)
        t_iter = []
        t_ln, _ = seq_ln(k, x4t, [zst[:, :]], zpr[:, :], scr, [t_x])
        def zq(v):
            v.tensor_scalar(zst[:, :], zst[:, :], s1b[:, 1:2], MAGIC, ALU.mult, ALU.add)
            return v.tensor_scalar(zqb[:, :], zst[:, :], -MAGIC, None, ALU.add)
        t_z = k.op("vector", zq, waits=[t_ln, t_s1])
        t_zT = seq_transpose(k, zqb, list(range(8)), lambda j: zqT[:, j*128:j*128+128],
                             ident_b[:, :], tps, mp[3], [t_z, t_const], evac_eng="scalar")
        for wi, (wT, stg, t_wrp) in enumerate(((wuT, ust, t_wr_u), (wgT, gst, t_wr_g))):
            t_ev = []
            for fb in range(6):
                pb = mp[(wi*6 + fb) % 3]
                t_mm = seq_mm(k, pb[:, :],
                              [zqT[:, cc*128:(cc+1)*128] for cc in range(NCC)],
                              [wT[:, cc*HID2 + fb*512: cc*HID2 + fb*512 + 512] for cc in range(NCC)],
                              [t_zT, t_wg] + ([t_ev[-3]] if len(t_ev) >= 3 else [t_mx]))
                def ef(s_, pb=pb, fb=fb, stg=stg):
                    return s_.activation(stg[:, fb*512:(fb+1)*512], pb[:, :], AF.Copy,
                                         scale=s1b[:, 0:1])
                t_ev.append(k.op("scalar", ef, waits=[t_mm, t_s1] + ([t_wrp] if t_wrp else [])))
            def mxf(v, wi=wi, stg=stg, first=(ci == 0)):
                i = v.tensor_reduce(scr[:, 8+wi:9+wi], stg[:, :], axis=AX.X, op=ALU.max,
                                    apply_absolute_value=True)
                if first:
                    return v.tensor_copy(out=mxa[:, wi:wi+1], in_=scr[:, 8+wi:9+wi])
                return v.tensor_tensor(mxa[:, wi:wi+1], mxa[:, wi:wi+1], scr[:, 8+wi:9+wi], ALU.max)
            t_mx = k.op("vector", mxf, waits=[t_ev[-1]])
            t_wr = k.dma("gpsimd", (up_raw if wi == 0 else gate_raw)[ci], stg[:, :],
                         waits=[t_ev[-1]])
            if wi == 0:
                t_wr_u = t_wr
            else:
                t_wr_g = t_wr
        t_iter = [t_mx]
    t_mt = k.op("tensor", lambda t: t.transpose(tps[0:2, 0:128], mxa[:, :], ident_f[:, :]),
                waits=[t_mx, t_wr_u, t_wr_g])
    def mrf(v):
        v.tensor_copy(out=zst[0:2, 0:128], in_=tps[0:2, 0:128])
        return v.tensor_reduce(zst[0:2, 128:129], zst[0:2, 0:128], axis=AX.X, op=ALU.max)
    t_mr = k.op("vector", mrf, waits=[t_mt])
    t_md = k.dma("gpsimd", mx_out, zst[0:2, 128:129], waits=[t_mr])
    k.wait_only("gpsimd", k.dma_drain("gpsimd") + k.dma_drain("sync"))


def build_P1(k, IN, es):
    nc = k.nc
    ugb = k.sbuf("p1_ug", [128, 4], F32, es=es)
    t_ug = gmax_reduce(k, IN["gmax_ug"], 2, ugb, es, "p1")
    gt = [k.sbuf(f"p1_g{i}", [128, HID2], F32, es=es) for i in range(2)]
    rt = [k.sbuf(f"p1_r{i}", [128, HID2], F32, es=es) for i in range(2)]
    sg = [k.sbuf(f"p1_s{i}", [128, HID2], F32, es=es) for i in range(2)]
    tt = [k.sbuf(f"p1_t{i}", [128, HID2], F32, es=es) for i in range(2)]
    scr = k.sbuf("p1_scr", [128, 4], F32, es=es)
    mxa = k.sbuf("p1_mx", [128, 1], F32, es=es)
    ident_f = k.sbuf("p1_idf", [128, 128], F32, es=es)
    def cinit(g):
        g.memset(ident_f[:, :], 1.0)
        return g.affine_select(ident_f[:, :], ident_f[:, :], pattern=[[1, 128]],
                               compare_op=ALU.is_equal, fill=0.0, base=0, channel_multiplier=-1)
    t_const = k.op("gpsimd", cinit)
    tps = k.psum("p1_tps", [128, 512], F32, es=es)

    gate_raw = IN["gate_raw"]
    t_raw = nc.dram_tensor("t_raw", [NCH, 128, HID2], F32, kind="ExternalOutput").ap()
    mx_out = nc.dram_tensor("gmax_t", [1, 1], F32, kind="ExternalOutput").ap()

    t_v = {-2: None, -1: None}
    t_wr = {-2: None, -1: None}
    t_ld = {}
    for ci in range(NCH):
        b = ci % 2
        t_ld[ci] = k.dma("sync", gt[b][:, :], gate_raw[ci],
                         waits=[t_v.get(ci - 2), t_ug if ci == 0 else None])
        def sf(s_, b=b):
            s_.activation(rt[b][:, :], gt[b][:, :], AF.Copy, scale=ugb[:, 3:4], bias=MAGIC)
            s_.activation(rt[b][:, :], rt[b][:, :], AF.Copy, bias=-MAGIC)
            s_.activation(sg[b][:, :], rt[b][:, :], AF.Sigmoid, scale=ugb[:, 1:2])
            return s_.activation(sg[b][:, :], sg[b][:, :], AF.Copy, scale=65536.0, bias=MAGIC)
        t_s = k.op("scalar", sf, waits=[t_ld[ci], t_ug])
        def vf(v, b=b, first=(ci == 0)):
            v.tensor_scalar(sg[b][:, :], sg[b][:, :], -MAGIC, 0.0, ALU.add, ALU.max)
            v.tensor_scalar(sg[b][:, :], sg[b][:, :], 65535.0, 2.0**-16, ALU.min, ALU.mult)
            v.tensor_scalar(rt[b][:, :], rt[b][:, :], ugb[:, 1:2], None, ALU.mult)
            v.tensor_tensor(tt[b][:, :], rt[b][:, :], sg[b][:, :], ALU.mult)
            i = v.tensor_reduce(scr[:, 0:1], tt[b][:, :], axis=AX.X, op=ALU.max,
                                apply_absolute_value=True)
            if first:
                return v.tensor_copy(out=mxa[:, :], in_=scr[:, 0:1])
            return v.tensor_tensor(mxa[:, :], mxa[:, :], scr[:, 0:1], ALU.max)
        t_v[ci] = k.op("vector", vf, waits=[t_s, t_wr.get(ci - 2)])
        t_wr[ci] = k.dma("gpsimd", t_raw[ci], tt[b][:, :], waits=[t_v[ci]])
    t_mt = k.op("tensor", lambda t: t.transpose(tps[0:1, 0:128], mxa[:, :], ident_f[:, :]),
                waits=[t_v[NCH - 1], t_const])
    def mrf(v):
        v.tensor_copy(out=scr[0:1, 0:1].tensor[0:1, 0:128] if False else gt[0][0:1, 0:128],
                      in_=tps[0:1, 0:128])
        return v.tensor_reduce(gt[0][0:1, 128:129], gt[0][0:1, 0:128], axis=AX.X, op=ALU.max)
    t_mr = k.op("vector", mrf, waits=[t_mt])
    k.dma("gpsimd", mx_out, gt[0][0:1, 128:129], waits=[t_mr])
    k.wait_only("gpsimd", k.dma_drain("gpsimd") + k.dma_drain("sync"))


def build_P2(k, IN, es):
    nc = k.nc
    ugb = k.sbuf("p2_ug", [128, 4], F32, es=es)
    t_ug = gmax_reduce(k, IN["gmax_ug"], 2, ugb, es, "p2u")
    s3b = k.sbuf("p2_s3", [128, 2], F32, es=es)
    t_s3 = gmax_reduce(k, IN["gmax_t"], 1, s3b, es, "p2t")
    cbt = k.sbuf("p2_c", [128, 1], F32, es=es)
    t_c = k.op("vector", lambda v: v.tensor_tensor(cbt[:, :], s3b[:, 0:1], ugb[:, 0:1], ALU.mult),
               waits=[t_ug, t_s3])
    tb = [k.sbuf(f"p2_t{i}", [128, HID2], F32, es=es) for i in range(2)]
    ub = [k.sbuf(f"p2_u{i}", [128, HID2], F32, es=es) for i in range(2)]
    scr = k.sbuf("p2_scr", [128, 4], F32, es=es)
    mxa = k.sbuf("p2_mx", [128, 1], F32, es=es)
    ident_f = k.sbuf("p2_idf", [128, 128], F32, es=es)
    def cinit(g):
        g.memset(ident_f[:, :], 1.0)
        return g.affine_select(ident_f[:, :], ident_f[:, :], pattern=[[1, 128]],
                               compare_op=ALU.is_equal, fill=0.0, base=0, channel_multiplier=-1)
    t_const = k.op("gpsimd", cinit)
    tps = k.psum("p2_tps", [128, 512], F32, es=es)

    t4_raw = nc.dram_tensor("t4_raw", [NCH, 128, HID2], F32, kind="ExternalOutput").ap()
    mx_out = nc.dram_tensor("gmax_t4", [1, 1], F32, kind="ExternalOutput").ap()

    t_v = {}
    t_wr = {}
    for ci in range(NCH):
        b = ci % 2
        t_l1 = k.dma("sync", tb[b][:, :], IN["t_raw"][ci],
                     waits=[t_v.get(ci - 2), t_wr.get(ci - 2), t_c if ci == 0 else None])
        t_l2 = k.dma("gpsimd", ub[b][:, :], IN["up_raw"][ci], waits=[t_v.get(ci - 2)])
        def sf(s_, b=b):
            s_.activation(tb[b][:, :], tb[b][:, :], AF.Copy, scale=s3b[:, 1:2], bias=MAGIC)
            s_.activation(tb[b][:, :], tb[b][:, :], AF.Copy, bias=-MAGIC)
            s_.activation(ub[b][:, :], ub[b][:, :], AF.Copy, scale=ugb[:, 2:3], bias=MAGIC)
            return s_.activation(ub[b][:, :], ub[b][:, :], AF.Copy, bias=-MAGIC)
        t_s = k.op("scalar", sf, waits=[t_l1, t_l2, t_ug, t_s3])
        def vf(v, b=b, first=(ci == 0)):
            v.tensor_tensor(tb[b][:, :], tb[b][:, :], ub[b][:, :], ALU.mult)
            v.tensor_scalar(tb[b][:, :], tb[b][:, :], cbt[:, 0:1], None, ALU.mult)
            i = v.tensor_reduce(scr[:, 0:1], tb[b][:, :], axis=AX.X, op=ALU.max,
                                apply_absolute_value=True)
            if first:
                return v.tensor_copy(out=mxa[:, :], in_=scr[:, 0:1])
            return v.tensor_tensor(mxa[:, :], mxa[:, :], scr[:, 0:1], ALU.max)
        t_v[ci] = k.op("vector", vf, waits=[t_s, t_c, t_wr.get(ci - 2)])
        t_wr[ci] = k.dma("gpsimd", t4_raw[ci], tb[b][:, :], waits=[t_v[ci]])
    t_mt = k.op("tensor", lambda t: t.transpose(tps[0:1, 0:128], mxa[:, :], ident_f[:, :]),
                waits=[t_v[NCH - 1], t_const, t_wr[NCH - 1], t_wr[NCH - 2]])
    def mrf(v):
        v.tensor_copy(out=tb[0][0:1, 0:128], in_=tps[0:1, 0:128])
        return v.tensor_reduce(tb[0][0:1, 128:129], tb[0][0:1, 0:128], axis=AX.X, op=ALU.max)
    t_mr = k.op("vector", mrf, waits=[t_mt])
    k.dma("gpsimd", mx_out, tb[0][0:1, 128:129], waits=[t_mr])
    k.wait_only("gpsimd", k.dma_drain("gpsimd") + k.dma_drain("sync"))


def build_P3(k, IN, es, x5_out):
    nc = k.nc
    ident_b = k.sbuf("p3_identb", [128, 128], BF16, es=es)
    ident_f = k.sbuf("p3_identf", [128, 128], F32, es=es)
    def cinit(g):
        g.memset(ident_b[:, :], 1.0)
        g.memset(ident_f[:, :], 1.0)
        g.affine_select(ident_b[:, :], ident_b[:, :], pattern=[[1, 128]], compare_op=ALU.is_equal,
                        fill=0.0, base=0, channel_multiplier=-1)
        return g.affine_select(ident_f[:, :], ident_f[:, :], pattern=[[1, 128]], compare_op=ALU.is_equal,
                               fill=0.0, base=0, channel_multiplier=-1)
    t_const = k.op("gpsimd", cinit)
    s5b = k.sbuf("p3_s5", [128, 2], F32, es=es)
    t_s5 = gmax_reduce(k, IN["gmax_t4"], 1, s5b, es, "p3")

    wdq_nat = nc.dram_tensor("wdq_nat", [C, HID2], BF16, kind="Internal").ap()
    t_qd, wqb_d = build_lpbq(k, IN["w_down"], C, HID2, wdq_nat, es, "d")
    wdT = k.sbuf("p3_wdT", [128, NOC * C], BF16, es=es)
    tps = k.psum("p3_tps", [128, 512], F32, es=es)
    mp = [k.psum(f"p3_mp{i}", [128, 512], F32, es=es) for i in range(3)]
    t_wd = load_wT(k, wdq_nat, C, HID2, wdT, wqb_d, ident_b, tps, mp[2], es, [t_qd, t_const], "d")

    t4t = [k.sbuf(f"p3_t4{i}", [128, HID2], F32, es=es) for i in range(2)]
    r5b = [k.sbuf(f"p3_r5{i}", [128, HID2], BF16, es=es) for i in range(2)]
    r5T = [k.sbuf(f"p3_r5T{i}", [128, HID2], BF16, es=es) for i in range(2)]
    x4t = [k.sbuf(f"p3_x4{i}", [128, C], F32, es=es) for i in range(2)]
    x5st = [k.sbuf(f"p3_x5{i}", [128, C], F32, es=es) for i in range(2)]

    t_done = {}
    t_wr = {}
    t_trl = None
    t_pe_prev = []
    for ci in range(NCH):
        b = ci % 2
        t_l1 = k.dma("sync", t4t[b][:, :], IN["t4_raw"][ci],
                     waits=[t_done.get(ci - 2), t_wd if ci == 0 else None])
        t_l2 = k.dma("gpsimd", x4t[b][:, :], chunk_src(IN["x4s"], ci), waits=[t_done.get(ci - 2)])
        def sf(s_, b=b):
            s_.activation(t4t[b][:, :], t4t[b][:, :], AF.Copy, scale=s5b[:, 1:2], bias=MAGIC)
            return s_.activation(r5b[b][:, :], t4t[b][:, :], AF.Copy, bias=-MAGIC)
        t_s = k.op("scalar", sf, waits=[t_l1, t_s5, t_done.get(ci - 2)])
        t_tr = seq_transpose(k, r5b[b], list(range(NOC)), lambda j, b=b: r5T[b][:, j*128:j*128+128],
                             ident_b[:, :], tps, mp[2], [t_s, t_const] + ([t_trl] if t_trl else []),
                             evac_eng="scalar")
        t_trl = t_tr
        t_pe = []
        for fb in range(2):
            pb = mp[fb]
            t_mm = seq_mm(k, pb[:, :],
                          [r5T[b][:, hc*128:(hc+1)*128] for hc in range(NOC)],
                          [wdT[:, hc*C + fb*512: hc*C + fb*512 + 512] for hc in range(NOC)],
                          [t_tr, t_wd] + ([t_pe_prev[fb]] if t_pe_prev else []))
            def ef(v, fb=fb, pb=pb, b=b):
                return v.scalar_tensor_tensor(x5st[b][:, fb*512:(fb+1)*512], pb[:, :],
                                              s5b[:, 0:1], x4t[b][:, fb*512:(fb+1)*512],
                                              ALU.mult, ALU.add)
            t_pe.append(k.op("vector", ef, waits=[t_mm, t_l2, t_s5, t_wr.get(ci - 2)]))
        t_wr[ci] = k.dma("gpsimd", chunk_src(x5_out, ci), x5st[b][:, :], waits=t_pe)
        t_done[ci] = t_pe[-1]
        t_pe_prev = t_pe
    k.wait_only("gpsimd", k.dma_drain("gpsimd") + k.dma_drain("sync"))


from concourse.bass_utils import run_bass_kernel_spmd

INTERNAL_STAGE = True
NCORES_K = 8


def _bacc():
    return Bacc("TRN2", target_bir_lowering=False, num_devices=8,
                detect_race_conditions=False, dynamic_dma_scratch_size=16384)


def _build_A():
    global INTERNAL_STAGE
    nc = _bacc()
    es = ExitStack()
    IN = {}
    def einp(name, shape):
        IN[name] = nc.dram_tensor(name, shape, F32, kind="ExternalInput").ap()
    einp("xin", [SEQ, T, C]); einp("w_qkv", [1024, 3072]); einp("b_qkv", [1, 3072])
    einp("w_proj", [1024, 1024]); einp("b_proj", [1, 1024]); einp("norm1_w", [1, 1024])
    einp("w1t", [1024, 4096]); einp("actsc", [1, 2]); einp("norm2_w", [1, 4096])
    einp("w2t", [4096, 1024])
    x3_out = nc.dram_tensor("x3", [SEQ, T, C], F32, kind="ExternalOutput").ap()
    build_phaseA(nc, es, IN, x3_out)
    es.close()
    nc.finalize()
    return nc


def _build_B1():
    nc = _bacc()
    es = ExitStack()
    IN = {}
    IN["x3g"] = nc.dram_tensor("x3g", [16, 256, 1024], F32, kind="ExternalInput").ap()
    IN["w_slow_p"] = nc.dram_tensor("w_slow_p", [1024, F_SLOW], F32, kind="ExternalInput").ap()
    IN["w_out"] = nc.dram_tensor("w_out", [1024, 1024], F32, kind="ExternalInput").ap()
    IN["ln_w"] = nc.dram_tensor("ln_w", [1, 1024], F32, kind="ExternalInput").ap()
    IN["ln_b"] = nc.dram_tensor("ln_b", [1, 1024], F32, kind="ExternalInput").ap()
    IN["masks"] = nc.dram_tensor("masks", [128, 384], BF16, kind="ExternalInput").ap()
    x4s = nc.dram_tensor("x4s", [16, 256, 1024], F32, kind="ExternalOutput").ap()
    zmax = nc.dram_tensor("zmax", [1, 1], F32, kind="ExternalOutput").ap()
    k = K(nc, es)
    build_phaseB1(k, IN, IN["x3g"], x4s, es, zmax_out=zmax)
    k.wait_only("gpsimd", k.dma_drain("gpsimd") + k.dma_drain("sync"))
    k.realize("b1")
    es.close()
    nc.finalize()
    return nc


def _build_P0():
    nc = _bacc()
    es = ExitStack()
    IN = {}
    IN["x4s"] = nc.dram_tensor("x4s", [16, 256, 1024], F32, kind="ExternalInput").ap()
    IN["gmax_z"] = nc.dram_tensor("gmax_z", [1, 8], F32, kind="ExternalInput").ap()
    IN["w_up"] = nc.dram_tensor("w_up", [HID2, C], F32, kind="ExternalInput").ap()
    IN["w_gate"] = nc.dram_tensor("w_gate", [HID2, C], F32, kind="ExternalInput").ap()
    k = K(nc, es)
    build_P0(k, IN, es)
    k.realize("p0")
    es.close()
    nc.finalize()
    return nc


def _build_P1():
    nc = _bacc()
    es = ExitStack()
    IN = {}
    IN["gate_raw"] = nc.dram_tensor("gate_raw_in", [NCH, 128, HID2], F32, kind="ExternalInput").ap()
    IN["gmax_ug"] = nc.dram_tensor("gmax_ug", [1, 16], F32, kind="ExternalInput").ap()
    k = K(nc, es)
    build_P1(k, IN, es)
    k.realize("p1")
    es.close()
    nc.finalize()
    return nc


def _build_P2():
    nc = _bacc()
    es = ExitStack()
    IN = {}
    IN["t_raw"] = nc.dram_tensor("t_raw_in", [NCH, 128, HID2], F32, kind="ExternalInput").ap()
    IN["up_raw"] = nc.dram_tensor("up_raw_in", [NCH, 128, HID2], F32, kind="ExternalInput").ap()
    IN["gmax_ug"] = nc.dram_tensor("gmax_ug", [1, 16], F32, kind="ExternalInput").ap()
    IN["gmax_t"] = nc.dram_tensor("gmax_t", [1, 8], F32, kind="ExternalInput").ap()
    k = K(nc, es)
    build_P2(k, IN, es)
    k.realize("p2")
    es.close()
    nc.finalize()
    return nc


def _build_P3():
    nc = _bacc()
    es = ExitStack()
    IN = {}
    IN["t4_raw"] = nc.dram_tensor("t4_raw_in", [NCH, 128, HID2], F32, kind="ExternalInput").ap()
    IN["x4s"] = nc.dram_tensor("x4s", [16, 256, 1024], F32, kind="ExternalInput").ap()
    IN["w_down"] = nc.dram_tensor("w_down", [C, HID2], F32, kind="ExternalInput").ap()
    IN["gmax_t4"] = nc.dram_tensor("gmax_t4", [1, 8], F32, kind="ExternalInput").ap()
    x5 = nc.dram_tensor("x5s", [16, 256, 1024], F32, kind="ExternalOutput").ap()
    k = K(nc, es)
    build_P3(k, IN, es, x5)
    k.realize("p3")
    es.close()
    nc.finalize()
    return nc


def _make_masks():
    import ml_dtypes
    m = np.zeros((128, 384), np.float32)
    for p in range(8):
        for t in range(16):
            for s in range(16):
                i, j = p*16+t, p*16+s
                if s < t:
                    m[i, j] = -1.0
                    m[j, 128+i] = -1.0
                if s <= t:
                    m[j, 256+i] = 1.0
    return m.astype(ml_dtypes.bfloat16)


def _slow_perm():
    perm = np.zeros(F_SLOW, np.int64)
    for h in range(H):
        perm[h*64:(h+1)*64] = h*193 + np.arange(64)
        perm[1024 + h*64:1024 + (h+1)*64] = h*193 + 64 + np.arange(64)
        perm[2048 + h*64:2048 + (h+1)*64] = h*193 + 128 + np.arange(64)
        perm[3072 + h] = h*193 + 192
    return perm


_PROGS = {}


def _progs():
    if not _PROGS:
        _PROGS["A"] = _build_A()
        _PROGS["B1"] = _build_B1()
        _PROGS["P0"] = _build_P0()
        _PROGS["P1"] = _build_P1()
        _PROGS["P2"] = _build_P2()
        _PROGS["P3"] = _build_P3()
    return _PROGS


def kernel(**inputs):
    P = _progs()
    cores = list(range(NCORES_K))
    x = inputs["L0_attn_x"]
    f32 = np.float32
    w1t = np.ascontiguousarray(inputs["L0_ffn_w1"].T.astype(f32))
    w2t = np.ascontiguousarray(inputs["L0_ffn_w2"].T.astype(f32))
    actsc = np.array([[inputs["L0_ffn_act_scale"].reshape(-1)[0],
                       inputs["L0_ffn_act_bias"].reshape(-1)[0]]], f32)
    ws_p = np.ascontiguousarray(inputs["L1_attn_w_slow"].astype(f32)[:, _slow_perm()])
    masks = _make_masks()

    imA = [{
        "xin": np.ascontiguousarray(x[2*c:2*c+2].astype(f32)),
        "w_qkv": inputs["L0_attn_w_qkv"].astype(f32),
        "b_qkv": inputs["L0_attn_b_qkv"].reshape(1, -1).astype(f32),
        "w_proj": inputs["L0_attn_w_proj"].astype(f32),
        "b_proj": inputs["L0_attn_b_proj"].reshape(1, -1).astype(f32),
        "norm1_w": inputs["L0_ffn_norm1_w"].reshape(1, -1).astype(f32),
        "w1t": w1t, "actsc": actsc,
        "norm2_w": inputs["L0_ffn_norm2_w"].reshape(1, -1).astype(f32),
        "w2t": w2t,
    } for c in cores]
    rA = run_bass_kernel_spmd(P["A"], imA, core_ids=cores)
    x3 = np.concatenate([rA.results[c]["x3"] for c in cores], axis=0)

    imB = [{
        "x3g": np.ascontiguousarray(x3[:, c*256:(c+1)*256, :]),
        "w_slow_p": ws_p,
        "w_out": inputs["L1_attn_w_out"].astype(f32),
        "ln_w": inputs["L1_attn_ln_w"].reshape(1, -1).astype(f32),
        "ln_b": inputs["L1_attn_ln_b"].reshape(1, -1).astype(f32),
        "masks": masks,
    } for c in cores]
    rB = run_bass_kernel_spmd(P["B1"], imB, core_ids=cores)
    gmax_z = np.concatenate([rB.results[c]["zmax"] for c in cores], axis=1)

    im0 = [{
        "x4s": rB.results[c]["x4s"], "gmax_z": gmax_z,
        "w_up": inputs["L1_ffn_w_up"].astype(f32),
        "w_gate": inputs["L1_ffn_w_gate"].astype(f32),
    } for c in cores]
    r0 = run_bass_kernel_spmd(P["P0"], im0, core_ids=cores)
    gmax_ug = np.concatenate([r0.results[c]["gmax_ug"] for c in cores], axis=1)

    im1 = [{"gate_raw_in": r0.results[c]["gate_raw"], "gmax_ug": gmax_ug} for c in cores]
    r1 = run_bass_kernel_spmd(P["P1"], im1, core_ids=cores)
    gmax_t = np.concatenate([r1.results[c]["gmax_t"] for c in cores], axis=1)

    im2 = [{"t_raw_in": r1.results[c]["t_raw"], "up_raw_in": r0.results[c]["up_raw"],
            "gmax_ug": gmax_ug, "gmax_t": gmax_t} for c in cores]
    r2 = run_bass_kernel_spmd(P["P2"], im2, core_ids=cores)
    gmax_t4 = np.concatenate([r2.results[c]["gmax_t4"] for c in cores], axis=1)

    im3 = [{"t4_raw_in": r2.results[c]["t4_raw"], "x4s": rB.results[c]["x4s"],
            "w_down": inputs["L1_ffn_w_down"].astype(f32), "gmax_t4": gmax_t4}
           for c in cores]
    r3 = run_bass_kernel_spmd(P["P3"], im3, core_ids=cores)
    x5 = np.concatenate([r3.results[c]["x5s"] for c in cores], axis=1)
    return np.ascontiguousarray(x5.astype(np.float32))
